# version 26
# baseline (speedup 1.0000x reference)
import numpy as np
from contextlib import ExitStack

import concourse.bass as bass
import concourse.mybir as mybir
from concourse.bass_utils import run_bass_kernel_spmd

F32 = mybir.dt.float32
BF16 = mybir.dt.bfloat16
AF = mybir.ActivationFunctionType
ALU = mybir.AluOpType

B = 8
N = 4096
C = 1024
CK = C // 128
W = 4096
WK = W // 128
NB_A = 512
NBA = N // NB_A
NB_B = 256
NBB = N // NB_B
NSUB = NB_A // 128
EPS_LN = 1e-5
EPS_Z = float(np.finfo(np.float32).eps)
NW2BUF = 4


class Sem:
    def __init__(self, stack, nc, name):
        self.h = stack.enter_context(nc.semaphore(name))
        self.name = name
        self.n = 0
        self.maxwait = 0

    def inc(self, inst, k=1):
        inst.then_inc(self.h, k)
        self.n += k
        return self.n


def WG(eng, sem, v):
    if v > 0:
        sem.maxwait = max(sem.maxwait, v)
        eng.wait_ge(sem.h, v)


def build_nc():
    nc = bass.Bass("TRN2", target_bir_lowering=False)

    xT = nc.dram_tensor("xT", [C, N], F32, kind="ExternalInput")
    wqkvT = nc.dram_tensor("wqkvT", [C, 3 * C], F32, kind="ExternalInput")
    wprojT = nc.dram_tensor("wprojT", [C, C], F32, kind="ExternalInput")
    bprojc_h = nc.dram_tensor("bprojc", [128, CK], F32, kind="ExternalInput")
    w1_h = nc.dram_tensor("w1", [C, W], F32, kind="ExternalInput")
    b1c_h = nc.dram_tensor("b1c", [128, WK], F32, kind="ExternalInput")
    w2r_h = nc.dram_tensor("w2r", [CK * 128, WK, 128], F32, kind="ExternalInput")
    b2c_h = nc.dram_tensor("b2c", [128, CK], F32, kind="ExternalInput")
    zsel_h = nc.dram_tensor("zsel", [16, CK, 128], F32, kind="ExternalInput")
    outT = nc.dram_tensor("outT", [C, N], F32, kind="ExternalOutput")
    qsc = nc.dram_tensor("qsc", [C, N], BF16, kind="Internal")
    w2bf = nc.dram_tensor("w2bf", [CK * 128, WK, 128], BF16, kind="Internal")

    xT_r = xT.ap().rearrange("(c p) n -> p c n", p=128)
    wqkvT_r = wqkvT.ap().rearrange("(c p) j -> p c j", p=128)
    wprojT_r = wprojT.ap().rearrange("(c p) j -> p c j", p=128)
    w1_r = w1_h.ap().rearrange("(c p) w -> p c w", p=128)
    qsc_r = qsc.ap().rearrange("(c p) n -> p c n", p=128)
    outT_r = outT.ap().rearrange("(c p) n -> p c n", p=128)

    st = ExitStack()
    with st:
        S = lambda name: Sem(st, nc, name)
        SB = lambda name, shape, dt: st.enter_context(nc.sbuf_tensor(name, shape, dt))
        w1_sb = SB("w1_sb", [128, CK, W], BF16)
        wproj_sb = SB("wproj_sb", [128, CK, C], BF16)
        ksmat = SB("ksmat", [128, CK, 16], BF16)
        ctxt = SB("ctxt", [128, CK, 128], BF16)
        bprojc = SB("bprojc_sb", [128, CK], F32)
        b1c = SB("b1c_sb", [128, WK], F32)
        b2c = SB("b2c_sb", [128, CK], F32)
        ones128 = SB("ones128", [128, 1], BF16)
        ones1b = SB("ones1b", [1, 128], BF16)

        s_ones = S("s_ones")
        s_wq = S("s_wq"); s_w1 = S("s_w1"); s_wp = S("s_wp")
        s_w2pc = S("s_w2pc"); s_cst = S("s_cst")
        s_xa = [S("s_xa0"), S("s_xa1")]
        s_sq = S("s_sq"); s_mmst = S("s_mmst"); s_rowv = S("s_rowv")
        s_rstd = S("s_rstd"); s_rown = S("s_rown"); s_mmbc = S("s_mmbc")
        s_pa1 = S("s_pa1"); s_lnx = S("s_lnx")
        s_mmq = S("s_mmq"); s_qev = S("s_qev")
        s_qst = [S("s_qst0"), S("s_qst1")]
        s_mmkv = S("s_mmkv"); s_kev = S("s_kev"); s_vev = S("s_vev")
        s_mmctx = S("s_mmctx"); s_ctxa = S("s_ctxa"); s_tail = S("s_tail")
        s_xb = [S(f"s_xb{r}") for r in range(3)]
        s_qb = [S(f"s_qb{r}") for r in range(2)]
        s_w2 = [S(f"s_w2_{r}") for r in range(NW2BUF)]
        s_mmzd = S("s_mmzd"); s_z = S("s_z"); s_mmzbc = S("s_mmzbc")
        s_qz = S("s_qz"); s_mmattn = S("s_mmattn"); s_aev = S("s_aev")
        s_mmproj = S("s_mmproj"); s_x1 = S("s_x1"); s_pev = S("s_pev")
        s_cast2 = S("s_cast2"); s_sq2 = S("s_sq2"); s_mmst2 = S("s_mmst2")
        s_rowv2 = S("s_rowv2"); s_rstd2 = S("s_rstd2"); s_rown2 = S("s_rown2")
        s_mmbc2 = S("s_mmbc2"); s_pa2 = S("s_pa2"); s_lnx2 = S("s_lnx2")
        s_mmf1 = S("s_mmf1"); s_hev = S("s_hev")
        s_mmf2 = S("s_mmf2"); s_oev = S("s_oev")
        s_ost = [S("s_ost0"), S("s_ost1")]
        s_vch = S("s_vch"); s_sch = S("s_sch")
        s_sqp = S("s_sqp"); s_sq2p = S("s_sq2p"); s_mz2 = S("s_mz2")
        s_mz1 = S("s_mz1"); s_zl = S("s_zl")
        s_rhl = S("s_rhl"); s_rhl2 = S("s_rhl2")

        allsems = []
        for v in list(locals().values()):
            if isinstance(v, Sem):
                allsems.append(v)
            elif isinstance(v, list) and v and isinstance(v[0], Sem):
                allsems.extend(v)

        nc.vector.memset(ones128[:, :], 1.0)
        nc.vector.memset(ones1b[:, :], 1.0)
        s_cst.inc(nc.sync.dma_start(bprojc[:, :], bprojc_h.ap()), 16)
        s_cst.inc(nc.sync.dma_start(b1c[:, :], b1c_h.ap()), 16)
        s_cst.inc(nc.sync.dma_start(b2c[:, :], b2c_h.ap()), 16)

        with ExitStack() as sta:
            SA = lambda name, shape, dt: sta.enter_context(nc.sbuf_tensor(name, shape, dt))
            ones1f = SA("ones1f", [1, 128], F32)
            s_ones.inc(nc.vector.memset(ones1f[:, :], 1.0))
            wqkv_sb = SA("wqkv_sb", [128, CK, 3 * C], BF16)
            xa = SA("xa", [128, 2, CK, NB_A], BF16)
            lnx = SA("lnx", [128, 2, CK, NB_A], BF16)
            qout = SA("qout", [128, 2, NB_A], BF16)
            kvt = SA("kvt", [128, 4, 2056], BF16)
            ctx_acc = SA("ctxacc", [128, 1032], F32)
            tmp = SA("tmpA", [128, NB_A], F32)
            xsq = SA("xsqA", [128, CK, NB_A], BF16)
            mean_row = SA("meanA", [1, NB_A], F32)[0:1, :]
            msq_row = SA("msqA", [1, NB_A], F32)[0:1, :]
            m2_row = SA("m2A", [1, NB_A], F32)[0:1, :]
            var_row = SA("varA", [1, NB_A], F32)[0:1, :]
            lnv_row = msq_row
            rstd_row = m2_row
            negm_row = var_row
            rhiA = SA("rhiA", [1, NB_A], BF16)[0:1, :]
            rloA = SA("rloA", [1, NB_A], BF16)[0:1, :]
            nhiA = SA("nhiA", [1, NB_A], BF16)[0:1, :]
            nloA = SA("nloA", [1, NB_A], BF16)[0:1, :]

            PS = lambda name: sta.enter_context(nc.psum_tensor(name, [128, 512], F32))
            PAq = [PS("PAq0"), PS("PAq1")]
            PAkv = [PS("PAkv0"), PS("PAkv1")]
            PActx = [PS("PActx0"), PS("PActx1"), PS("PActx2")]
            PAst = PS("PAst")
            CTXB = [(0, 3), (3, 3), (6, 2)]

            nc.vector.memset(PActx[0][:, 129:387], 0.0)
            nc.vector.memset(PActx[1][:, 129:387], 0.0)
            s_ctxa.inc(nc.vector.memset(PActx[2][:, 129:258], 0.0))
            s_ones.inc(nc.vector.memset(
                kvt[:, :, 1024:2056].rearrange("p s (pr x) -> p s pr x", x=129)[:, :, :, 128:129],
                1.0))

            s_wq.inc(nc.gpsimd.dma_start(wqkv_sb[:, :, :], wqkvT_r), 16)
            s_w1.inc(nc.gpsimd.dma_start(w1_sb[:, :, :], w1_r), 16)
            s_wp.inc(nc.gpsimd.dma_start(wproj_sb[:, :, :], wprojT_r), 16)
            for d in range(CK):
                s_w2pc.inc(nc.gpsimd.dma_start(
                    w2bf.ap()[d * 128:(d + 1) * 128, :, :],
                    w2r_h.ap()[d * 128:(d + 1) * 128, :, :]), 16)

            def emit_ctx_block(ii):
                WG(nc.tensor, s_kev, 8 * ii + 8)
                WG(nc.tensor, s_vev, 8 * ii + 8)
                WG(nc.tensor, s_ctxa, ii + 1)
                if ii == 0:
                    WG(nc.tensor, s_ones, 2)
                for q, (p0, np_) in enumerate(CTXB):
                    for pi in range(np_):
                        for s in range(NSUB):
                            g = 4 * ii + s
                            sl = g % 4
                            p = p0 + pi
                            ctx_mm = nc.tensor.matmul(
                                PActx[q][:, pi * 129:(pi + 1) * 129],
                                kvt[:, sl, p * 128:(p + 1) * 128],
                                kvt[:, sl, 1024 + p * 129:1024 + (p + 1) * 129],
                                start=(pi == 0 and s == 0), stop=(pi == np_ - 1 and s == NSUB - 1),
                                skip_group_check=True)
                s_mmctx.inc(ctx_mm)

            def emit_ctx_acc_block(ii):
                WG(nc.vector, s_mmctx, ii + 1)
                last = None
                for q, (p0, np_) in enumerate(CTXB):
                    w_ = np_ * 129
                    dst = ctx_acc[:, p0 * 129:p0 * 129 + w_]
                    if ii == 0:
                        last = nc.vector.tensor_copy(dst, PActx[q][:, 0:w_])
                    else:
                        last = nc.vector.tensor_tensor(
                            dst, dst, PActx[q][:, 0:w_], ALU.add)
                s_vch.inc(last)
                WG(nc.vector, s_vch, s_vch.n)
                nc.vector.memset(PActx[0][:, 129:387], 0.0)
                nc.vector.memset(PActx[1][:, 129:387], 0.0)
                s_ctxa.inc(nc.vector.memset(PActx[2][:, 129:258], 0.0))

            for i in range(NBA + 1):
                if i == 0:
                    s_xa[0].inc(nc.gpsimd.dma_start(xa[:, 0, :, :], xT_r[:, :, 0:NB_A]), 16)
                    s_xa[1].inc(nc.gpsimd.dma_start(xa[:, 1, :, :], xT_r[:, :, NB_A:2 * NB_A]), 16)
                elif i + 1 < NBA:
                    WG(nc.gpsimd, s_pa1, 8 * i)
                    WG(nc.gpsimd, s_sqp, 8 * i)

                    s_xa[(i + 1) % 2].inc(nc.gpsimd.dma_start(
                        xa[:, (i + 1) % 2, :, :], xT_r[:, :, (i + 1) * NB_A:(i + 2) * NB_A]), 16)

                def emit_squaresA(i2):
                    for c in range(CK):
                        WG(nc.vector, s_xa[i2 % 2], 16 * (i2 // 2 + 1))
                        WG(nc.vector, s_mmst, 8 * (i2 - 1) + c + 1)
                        s_sqp.inc(nc.vector.tensor_tensor(
                            xsq[:, c, :], xa[:, i2 % 2, c, :], xa[:, i2 % 2, c, :], ALU.mult))

                def emit_statsA(i2):
                    for c in range(CK):
                        if c == 0:
                            WG(nc.tensor, s_ones, 1)
                            WG(nc.tensor, s_rowv, i2)
                            WG(nc.tensor, s_mz1, i2 + 1)
                            WG(nc.tensor, s_xa[i2 % 2], 16 * (i2 // 2 + 1))
                        nc.tensor.matmul(PAst[0:1, 0:NB_A], ones128[:, :], xa[:, i2 % 2, c, :],
                                         start=(c == 0), stop=False,
                                         skip_group_check=True)
                    for c in range(CK):
                        WG(nc.tensor, s_sqp, 8 * i2 + c + 1)
                        s_mmst.inc(nc.tensor.matmul(PAst[32:33, 0:NB_A], ones128[:, :], xsq[:, c, :],
                                                    start=False, stop=(c == CK - 1),
                                                    tile_position=(0, 32),
                                                    skip_group_check=True))

                if i == 0:
                    s_mz1.inc(nc.scalar.memzero(PAst[32:33, 0:NB_A]))
                    emit_squaresA(0)
                    emit_statsA(0)

                if i < NBA:
                    WG(nc.vector, s_mmst, 8 * (i + 1))
                    nc.vector.tensor_scalar(mean_row, PAst[0:1, 0:NB_A], 1.0 / C, None, ALU.mult)
                    s_vch.inc(nc.vector.tensor_scalar(msq_row, PAst[32:33, 0:NB_A], 1.0 / C, None, ALU.mult))
                    WG(nc.vector, s_vch, s_vch.n)
                    s_vch.inc(nc.vector.tensor_tensor(m2_row, mean_row, mean_row, ALU.mult))
                    WG(nc.vector, s_vch, s_vch.n)
                    s_rowv.inc(nc.vector.scalar_tensor_tensor(
                        var_row, msq_row, EPS_LN, m2_row, ALU.add, ALU.subtract))
                    WG(nc.scalar, s_rowv, i + 1)
                    s_sch.inc(nc.scalar.activation(lnv_row, var_row, AF.Ln))
                    WG(nc.scalar, s_sch, s_sch.n)
                    s_rstd.inc(nc.scalar.activation(rstd_row, lnv_row, AF.Exp, scale=-0.5))
                    if i + 1 <= NBA - 1:
                        s_mz1.inc(nc.scalar.memzero(PAst[32:33, 0:NB_A]))
                    WG(nc.vector, s_rstd, i + 1)
                    s_rown.inc(nc.vector.scalar_tensor_tensor(
                        negm_row, mean_row, -1.0, rstd_row, ALU.mult, ALU.mult))
                if i >= 1:
                    ii = i - 1
                    for j in range(CK):
                        if j == 0:
                            WG(nc.tensor, s_lnx, 8 * ii + 8)
                            if ii == 0:
                                WG(nc.tensor, s_wq, 16)
                        elif j >= 2:
                            WG(nc.tensor, s_qev, 8 * ii + j - 1)
                        for c in range(CK):
                            q_mm = nc.tensor.matmul(PAq[j % 2][:, 0:NB_A],
                                                    wqkv_sb[:, c, j * 128:(j + 1) * 128],
                                                    lnx[:, ii % 2, c, :],
                                                    start=(c == 0), stop=(c == CK - 1))
                        s_mmq.inc(q_mm)
                        WG(nc.scalar, s_mmq, 8 * ii + j + 1)
                        kq = 8 * ii + j
                        WG(nc.scalar, s_qst[j % 2], 16 * (kq // 2))
                        s_qev.inc(nc.scalar.activation(qout[:, j % 2, :], PAq[j % 2][:, 0:NB_A], AF.Relu))
                        WG(nc.sync, s_qev, 8 * ii + j + 1)
                        s_qst[j % 2].inc(nc.sync.dma_start(
                            qsc_r[:, j, ii * NB_A:(ii + 1) * NB_A], qout[:, j % 2, :]), 16)

                if i < NBA:
                    WG(nc.tensor, s_rstd, i + 1)
                    WG(nc.tensor, s_qev, 8 * (i - 1) + 7)
                    s_mmbc.inc(nc.tensor.matmul(PAq[0][:, 0:NB_A], ones1f[:, :], rstd_row,
                                                start=True, stop=True))
                    WG(nc.tensor, s_rown, i + 1)
                    WG(nc.tensor, s_qev, 8 * (i - 1) + 8)
                    s_mmbc.inc(nc.tensor.matmul(PAq[1][:, 0:NB_A], ones1f[:, :], negm_row,
                                                start=True, stop=True))

                if i < NBA:
                    for c in range(CK):
                        if c == 0:
                            WG(nc.vector, s_mmbc, 2 * i + 2)
                            WG(nc.vector, s_mmkv, 16 * (i - 1))
                        else:
                            WG(nc.vector, s_lnx, 8 * i + c)
                        s_pa1.inc(nc.vector.tensor_tensor(
                            tmp[:, :], xa[:, i % 2, c, :], PAq[0][:, 0:NB_A], ALU.mult))
                        WG(nc.vector, s_pa1, 8 * i + c + 1)
                        s_lnx.inc(nc.vector.tensor_tensor(
                            lnx[:, i % 2, c, :], tmp[:, :], PAq[1][:, 0:NB_A], ALU.add))

                if 1 <= i + 1 <= NBA - 1:
                    emit_squaresA(i + 1)

                if i >= 1:
                    ii = i - 1
                    for s in range(NSUB):
                        g = 4 * ii + s
                        sl = g % 4
                        for idx, jt in enumerate([0, 1, 2, 3]):
                            bank = idx % 2
                            if idx == 0:
                                WG(nc.tensor, s_vev, 2 * g - 1)
                            elif idx == 1:
                                WG(nc.tensor, s_vev, 2 * g)
                            elif idx == 2:
                                WG(nc.tensor, s_kev, 2 * g + 1)
                            else:
                                WG(nc.tensor, s_kev, 2 * g + 2)
                            for c in range(CK):
                                kv_mm = nc.tensor.matmul(
                                    PAkv[bank][:, 0:NB_A],
                                    lnx[:, ii % 2, c, s * 128:(s + 1) * 128],
                                    wqkv_sb[:, c, C + jt * 512:C + (jt + 1) * 512],
                                    start=(c == 0), stop=(c == CK - 1))
                            s_mmkv.inc(kv_mm)
                            WG(nc.scalar, s_mmkv, 4 * g + idx + 1)
                            WG(nc.scalar, s_mmctx, ii)
                            if jt < 2:
                                s_kev.inc(nc.scalar.activation(
                                    kvt[:, sl, jt * 512:(jt + 1) * 512],
                                    PAkv[bank][:, 0:NB_A], AF.Relu))
                            else:
                                vv = kvt[:, sl, 1024:2056].rearrange(
                                    "p (pr x) -> p pr x", x=129)[:, (jt - 2) * 4:(jt - 1) * 4, 0:128]
                                s_vev.inc(nc.scalar.activation(
                                    vv, PAkv[bank][:, 0:NB_A].rearrange("p (a b) -> p a b", b=128),
                                    AF.Copy))
                    emit_ctx_block(ii)
                    emit_ctx_acc_block(ii)

                if i + 1 <= NBA - 1 and i >= 0:
                    emit_statsA(i + 1)


            nc.gpsimd.memset(ksmat[:, :, :], 0.0)
            s_tail.inc(nc.gpsimd.memset(ctxt[:, :, :], 0.0))
            WG(nc.vector, s_tail, 1)
            WG(nc.vector, s_ctxa, NBA + 1)
            for p in range(CK):
                base = p * 129
                nc.vector.tensor_copy(ksmat[0:64, p, 2 * p:2 * p + 1],
                                      ctx_acc[0:64, base + 128:base + 129])
                nc.vector.tensor_copy(ksmat[64:128, p, 2 * p + 1:2 * p + 2],
                                      ctx_acc[64:128, base + 128:base + 129])
                nc.vector.tensor_copy(ctxt[0:64, p, 0:64],
                                      ctx_acc[0:64, base:base + 64])
                nc.vector.tensor_copy(ctxt[64:128, p, 64:128],
                                      ctx_acc[64:128, base + 64:base + 128])
            nc.all_engine_barrier()

        with ExitStack() as stb:
            SBB = lambda name, shape, dt: stb.enter_context(nc.sbuf_tensor(name, shape, dt))
            projtmp = SBB("projtmp", [128, CK, NB_B], F32)
            qb = SBB("qb", [128, 2, CK, NB_B], BF16)
            x1 = SBB("x1", [128, 2, CK, NB_B], F32)
            zbcs = SBB("zbcs", [128, 2, NB_B], F32)
            ln2 = SBB("ln2", [128, 2, CK, NB_B], BF16)
            attnT = SBB("attnT", [128, CK, NB_B], BF16)
            hT = SBB("hT", [128, WK, NB_B], BF16)
            outb = SBB("outb", [128, CK, NB_B], F32)
            w2t = SBB("w2t", [128, NW2BUF, WK, 128], BF16)
            tmp2 = SBB("tmp2", [128, NB_B], F32)
            xsq2 = SBB("xsq2", [128, CK, NB_B], BF16)
            xbf2 = SBB("xbf2", [128, CK, NB_B], BF16)
            ztmp = SBB("ztmp", [16, NB_B], F32)
            zrow = SBB("zrow", [16, NB_B], F32)
            mean2 = SBB("mean2", [1, NB_B], F32)
            msq2 = SBB("msq2", [1, NB_B], F32)
            m22 = SBB("m22", [1, NB_B], F32)
            var2 = SBB("var2", [1, NB_B], F32)
            lnv2 = msq2
            rstd2 = m22
            negm2 = var2
            rhiB = SBB("rhiB", [1, NB_B], BF16)[0:1, :]
            rloB = SBB("rloB", [1, NB_B], BF16)[0:1, :]
            nhiB = SBB("nhiB", [1, NB_B], BF16)[0:1, :]
            nloB = SBB("nloB", [1, NB_B], BF16)[0:1, :]
            zsel_sb = SBB("zsel_sb", [16, CK, 128], BF16)
            zhi = SBB("zhi", [16, NB_B], BF16)
            zlo = SBB("zlo", [16, NB_B], BF16)
            s_zl.inc(nc.gpsimd.dma_start(zsel_sb[:, :, :], zsel_h.ap()), 16)

            PS2 = lambda name: stb.enter_context(nc.psum_tensor(name, [128, 512], F32))
            PB = [PS2(f"PB{k}") for k in range(8)]
            NB = NB_B

            def emit_rowpre2(bb):
                WG(nc.vector, s_mmst2, 8 * (bb + 1))
                nc.vector.tensor_scalar(mean2[:, :], PB[7][0:1, 0:NB], 1.0 / C, None, ALU.mult)
                s_vch.inc(nc.vector.tensor_scalar(msq2[:, :], PB[7][32:33, 0:NB], 1.0 / C, None, ALU.mult))
                WG(nc.vector, s_vch, s_vch.n)
                s_vch.inc(nc.vector.tensor_tensor(m22[:, :], mean2[:, :], mean2[:, :], ALU.mult))
                WG(nc.vector, s_vch, s_vch.n)
                s_rowv2.inc(nc.vector.scalar_tensor_tensor(
                    var2[:, :], msq2[:, :], EPS_LN, m22[:, :], ALU.add, ALU.subtract))

            def emit_rowpre2_act(bb):
                WG(nc.scalar, s_rowv2, bb + 1)
                s_sch.inc(nc.scalar.activation(lnv2[:, :], var2[:, :], AF.Ln))
                WG(nc.scalar, s_sch, s_sch.n)
                s_rstd2.inc(nc.scalar.activation(rstd2[:, :], lnv2[:, :], AF.Exp, scale=-0.5))

            def emit_negm2(bb):
                WG(nc.vector, s_rstd2, bb + 1)
                s_vch.inc(nc.vector.scalar_tensor_tensor(
                    negm2[:, :], mean2[:, :], -1.0, rstd2[:, :], ALU.mult, ALU.mult))
                WG(nc.vector, s_vch, s_vch.n)
                s_vch.inc(nc.vector.tensor_copy(rhiB, rstd2[:, :]))
                WG(nc.vector, s_vch, s_vch.n)
                s_vch.inc(nc.vector.tensor_tensor(rloB, rstd2[:, :], rhiB, ALU.subtract))
                WG(nc.vector, s_vch, s_vch.n)
                s_vch.inc(nc.vector.tensor_copy(nhiB, negm2[:, :]))
                WG(nc.vector, s_vch, s_vch.n)
                s_rhl2.inc(nc.vector.tensor_tensor(nloB, negm2[:, :], nhiB, ALU.subtract))

            def emit_zchain(b2):
                WG(nc.vector, s_mmzd, b2 + 1)
                s_vch.inc(nc.vector.tensor_scalar(ztmp[:, :], PB[0][0:16, 0:NB], EPS_Z, None, ALU.add))
                WG(nc.vector, s_vch, s_vch.n)
                s_vch.inc(nc.vector.reciprocal(zrow[:, :], ztmp[:, :]))
                WG(nc.vector, s_vch, s_vch.n)
                s_vch.inc(nc.vector.tensor_copy(zhi[:, :], zrow[:, :]))
                WG(nc.vector, s_vch, s_vch.n)
                s_z.inc(nc.vector.tensor_tensor(zlo[:, :], zrow[:, :], zhi[:, :], ALU.subtract))

            def emit_zden(b2):
                for c in range(CK):
                    if c == 0:
                        WG(nc.tensor, s_qb[b2 % 2], 16 * (b2 // 2 + 1))
                        WG(nc.tensor, s_hev, 32 * (b2 - 2) + 31)
                        WG(nc.tensor, s_aev, 8 * (b2 - 1) + 8)
                    zd_mm = nc.tensor.matmul(PB[0][0:16, 0:NB], ksmat[:, c, :], qb[:, b2 % 2, c, :],
                                             start=(c == 0), stop=(c == CK - 1))
                s_mmzd.inc(zd_mm)


            for b in range(NBB + 1):
                if b == 0:
                    s_mz2.inc(nc.scalar.memzero(PB[7][32:33, 0:NB]))
                    WG(nc.sync, s_qst[0], 16 * (4 * NBA))
                    WG(nc.sync, s_qst[1], 16 * (4 * NBA))
                    for t in range(2):
                        s_xb[t].inc(nc.sync.dma_start(
                            x1[:, t, :, :], xT_r[:, :, t * NB:(t + 1) * NB]), 16)
                    for t in range(2):
                        s_qb[t].inc(nc.sync.dma_start(
                            qb[:, t, :, :], qsc_r[:, :, t * NB:(t + 1) * NB]), 16)
                else:
                    if b + 1 < NBB:
                        t = b + 1
                        WG(nc.sync, s_mmattn, 8 * b)
                        WG(nc.sync, s_qst[0], 16 * 4 * (t // 2 + 1))
                        WG(nc.sync, s_qst[1], 16 * 4 * (t // 2 + 1))
                        s_qb[t % 2].inc(nc.sync.dma_start(
                            qb[:, t % 2, :, :], qsc_r[:, :, t * NB:(t + 1) * NB]), 16)
                def emit_w2loads(bb, parity):
                    for d in range(parity, CK, 2):
                        eng = nc.sync if d % 2 == 0 else nc.gpsimd
                        if bb == 0 and d <= 1:
                            WG(eng, s_w2pc, 16 * CK)
                        WG(eng, s_mmf2, 8 * bb + d - 3)
                        s_w2[d % 4].inc(eng.dma_start(
                            w2t[:, d % 4, :, :],
                            w2bf.ap()[d * 128:(d + 1) * 128, :, :]), 16)
                if b >= 1:
                    emit_w2loads(b - 1, 0)

                def emit_squaresB(b2):
                    for c in range(CK):
                        WG(nc.gpsimd, s_x1, 8 * (b2 + 1))
                        WG(nc.gpsimd, s_mmst2, 8 * (b2 - 1) + c + 1)
                        s_cast2.inc(nc.gpsimd.tensor_copy(xbf2[:, c, :], x1[:, b2 % 2, c, :]))
                        WG(nc.vector, s_x1, 8 * (b2 + 1))
                        WG(nc.vector, s_mmst2, 8 * (b2 - 1) + c + 1)
                        s_sq2p.inc(nc.vector.tensor_tensor(
                            xsq2[:, c, :], x1[:, b2 % 2, c, :], x1[:, b2 % 2, c, :], ALU.mult))

                def emit_statsB(b2):
                    for c in range(CK):
                        if c == 0:
                            WG(nc.tensor, s_rowv2, b2)
                            WG(nc.tensor, s_x1, 8 * (b2 + 1))
                            WG(nc.tensor, s_mz2, b2 + 1)
                        WG(nc.tensor, s_cast2, 8 * b2 + c + 1)
                        nc.tensor.matmul(PB[7][0:1, 0:NB], ones128[:, :], xbf2[:, c, :],
                                         start=(c == 0), stop=False,
                                         skip_group_check=True)
                    for c in range(CK):
                        WG(nc.tensor, s_sq2p, 8 * b2 + c + 1)
                        s_mmst2.inc(nc.tensor.matmul(PB[7][32:33, 0:NB], ones128[:, :], xsq2[:, c, :],
                                                     start=False, stop=(c == CK - 1),
                                                     tile_position=(0, 32),
                                                     skip_group_check=True))

                if b < NBB:
                    if b == 0:
                        emit_zden(0)
                        emit_zchain(0)
                    ABANK = (4, 0, 1)
                    ZBANK = (5, 2, 3)

                    def emit_attn(p):
                        pa = ABANK[p % 3]
                        if p == 0:
                            WG(nc.tensor, s_qb[b % 2], 16 * (b // 2 + 1))
                            WG(nc.tensor, s_lnx2, 8 * b)
                        elif p == 1:
                            WG(nc.tensor, s_hev, 32 * (b - 2) + 31)
                        elif p == 2:
                            WG(nc.tensor, s_hev, 32 * (b - 2) + 32)
                        else:
                            WG(nc.tensor, s_aev, 8 * b + p - 2)
                        s_mmattn.inc(nc.tensor.matmul(PB[pa][:, 0:NB], ctxt[:, p, :],
                                                      qb[:, b % 2, p, :], start=True, stop=True))

                    def emit_zbc(p):
                        pz = ZBANK[p % 3]
                        if p == 0:
                            WG(nc.tensor, s_z, b + 1)
                            WG(nc.tensor, s_lnx2, 8 * b)
                            if b == 0:
                                WG(nc.tensor, s_zl, 16)
                        elif p == 1:
                            WG(nc.tensor, s_oev, 8 * (b - 2) + 7)
                        elif p == 2:
                            WG(nc.tensor, s_oev, 8 * (b - 2) + 8)
                        else:
                            WG(nc.tensor, s_qz, 8 * b + p - 2)
                        nc.tensor.matmul(PB[pz][:, 0:NB], zsel_sb[:, p, :],
                                         zhi[:, :], start=True, stop=False)
                        s_mmzbc.inc(nc.tensor.matmul(PB[pz][:, 0:NB], zsel_sb[:, p, :],
                                                     zlo[:, :], start=False, stop=True))

                    for p in range(CK):
                        emit_zbc(p)
                        emit_attn(p)
                    for p in range(CK):
                        if p <= 1:
                            WG(nc.scalar, s_aev, 8 * b + p - 1)
                        else:
                            WG(nc.scalar, s_aev, 8 * b + p - 1)
                        WG(nc.scalar, s_mmzbc, 8 * b + p + 1)
                        s_qz.inc(nc.scalar.activation(zbcs[:, p % 2, :],
                                                      PB[ZBANK[p % 3]][:, 0:NB], AF.Copy))
                        WG(nc.vector, s_mmattn, 8 * b + p + 1)
                        WG(nc.vector, s_qz, 8 * b + p + 1)
                        if p == 0:
                            WG(nc.vector, s_mmproj, 8 * b)
                        s_aev.inc(nc.vector.tensor_tensor(
                            attnT[:, p, :], PB[ABANK[p % 3]][:, 0:NB], zbcs[:, p % 2, :], ALU.mult))

                def emit_bcast2(bb):
                    WG(nc.tensor, s_rhl2, bb + 1)
                    WG(nc.tensor, s_lnx2, 8 * bb)
                    WG(nc.tensor, s_aev, 8 * bb + 7)
                    WG(nc.tensor, s_qz, 8 * bb + 7)
                    nc.tensor.matmul(PB[4][:, 0:NB], ones1b[:, :], rhiB,
                                     start=True, stop=False)
                    s_mmbc2.inc(nc.tensor.matmul(PB[4][:, 0:NB], ones1b[:, :], rloB,
                                                 start=False, stop=True))
                    nc.tensor.matmul(PB[5][:, 0:NB], ones1b[:, :], nhiB,
                                     start=True, stop=False)
                    s_mmbc2.inc(nc.tensor.matmul(PB[5][:, 0:NB], ones1b[:, :], nloB,
                                                 start=False, stop=True))

                if b < NBB:
                    for j in range(CK):
                        if j == 0:
                            WG(nc.tensor, s_pev, 8 * (b - 1) + 7)
                            if b == 0:
                                WG(nc.tensor, s_wp, 16)
                        elif j == 1:
                            WG(nc.tensor, s_rowv2, b)
                            WG(nc.tensor, s_mz2, b + 1)
                            WG(nc.tensor, s_pev, 8 * (b - 1) + 8)
                        else:
                            WG(nc.tensor, s_pev, 8 * b + j - 1)
                        for d in range(CK):
                            if j == 0:
                                WG(nc.tensor, s_aev, 8 * b + d + 1)
                            pj_mm = nc.tensor.matmul(PB[6 + j % 2][:, 0:NB],
                                                     wproj_sb[:, d, j * 128:(j + 1) * 128],
                                                     attnT[:, d, :],
                                                     start=(d == 0), stop=(d == CK - 1))
                        s_mmproj.inc(pj_mm)
                        WG(nc.scalar, s_mmproj, 8 * b + j + 1)
                        WG(nc.scalar, s_x1, 8 * (b - 1) + j + 1)
                        s_pev.inc(nc.scalar.activation(projtmp[:, j, :],
                                                       PB[6 + j % 2][:, 0:NB], AF.Copy))
                def emit_pass2(bb, k):
                    if k == 0:
                        WG(nc.vector, s_mmbc2, 2 * bb + 2)
                        WG(nc.vector, s_mmf1, 32 * (bb - 1))
                    else:
                        WG(nc.vector, s_lnx2, 8 * bb + k)
                    s_pa2.inc(nc.vector.tensor_tensor(
                        tmp2[:, :], x1[:, bb % 2, k, :], PB[4][:, 0:NB], ALU.mult))
                    WG(nc.vector, s_pa2, 8 * bb + k + 1)
                    s_lnx2.inc(nc.vector.tensor_tensor(
                        ln2[:, bb % 2, k, :], tmp2[:, :], PB[5][:, 0:NB], ALU.add))

                def emit_x1(b2, k):
                    WG(nc.vector, s_pev, 8 * b2 + k + 1)
                    if k == 0:
                        WG(nc.vector, s_xb[b2 % 2], 16 * (b2 // 2 + 1))
                        if b2 == 0:
                            WG(nc.vector, s_cst, 48)
                    s_x1.inc(nc.vector.scalar_tensor_tensor(
                        x1[:, b2 % 2, k, :], projtmp[:, k, :], bprojc[:, k:k + 1],
                        x1[:, b2 % 2, k, :], ALU.add, ALU.add))

                if b < NBB:
                    for k in range(CK):
                        emit_x1(b, k)
                if b == 0:
                    emit_zden(1)
                    emit_zchain(1)
                if b < NBB:
                    emit_squaresB(b)
                if b >= 1:
                    emit_w2loads(b - 1, 1)

                def emit_rows(b2):
                    emit_rowpre2(b2)
                    emit_rowpre2_act(b2)
                    if b2 + 1 < NBB:
                        s_mz2.inc(nc.scalar.memzero(PB[7][32:33, 0:NB]))
                    emit_negm2(b2)

                if b == 0:
                    emit_statsB(0)
                    emit_rows(0)
                    emit_bcast2(0)
                    for k in range(CK):
                        emit_pass2(0, k)

                if b >= 1:
                    bb = b - 1
                    for w in range(WK):
                        if w == 0:
                            WG(nc.tensor, s_aev, 8 * b + 8 if b < NBB else s_aev.n)
                            if bb == 0:
                                WG(nc.tensor, s_w1, 16)
                        elif w == 1:
                            WG(nc.tensor, s_aev, 8 * b + 6 if b < NBB else s_aev.n)
                        else:
                            WG(nc.tensor, s_hev, 32 * bb + w - 1)
                        for c in range(CK):
                            if w == 0:
                                WG(nc.tensor, s_lnx2, 8 * bb + c + 1)
                            f1_mm = nc.tensor.matmul(PB[w % 2][:, 0:NB],
                                                     w1_sb[:, c, w * 128:(w + 1) * 128],
                                                     ln2[:, bb % 2, c, :],
                                                     start=(c == 0), stop=(c == CK - 1))
                        s_mmf1.inc(f1_mm)
                        WG(nc.scalar, s_mmf1, 32 * bb + w + 1)
                        if w == 0:
                            WG(nc.scalar, s_mmf2, 8 * bb)
                            if bb == 0:
                                WG(nc.scalar, s_cst, 48)
                        s_hev.inc(nc.scalar.activation(hT[:, w, :], PB[w % 2][:, 0:NB],
                                                       AF.Relu, bias=b1c[:, w:w + 1]))
                        if b < NBB:
                            if w == 11:
                                emit_statsB(b)
                            elif w == 13:
                                emit_rows(b)
                            elif w == 19:
                                emit_bcast2(b)
                            elif w == 21:
                                for k in range(CK):
                                    emit_pass2(b, k)
                    if b + 1 < NBB:
                        emit_zden(b + 1)
                        emit_zchain(b + 1)
                    for d in range(CK):
                        if d == 0:
                            WG(nc.tensor, s_hev, 32 * bb + 32)
                            WG(nc.tensor, s_qz, 8 * b + 8 if b < NBB else s_qz.n)
                        elif d == 1 and b < NBB:
                            WG(nc.tensor, s_qz, 8 * b + 6)
                        WG(nc.tensor, s_w2[d % 4], 16 * (2 * bb + d // 4 + 1))
                        WG(nc.tensor, s_oev, 8 * bb + d - 1)
                        for w in range(WK):
                            f2_mm = nc.tensor.matmul(PB[2 + d % 2][:, 0:NB],
                                                     w2t[:, d % 4, w, :],
                                                     hT[:, w, :],
                                                     start=(w == 0), stop=(w == WK - 1))
                        s_mmf2.inc(f2_mm)
                        WG(nc.vector, s_mmf2, 8 * bb + d + 1)
                        if d == 0:
                            WG(nc.vector, s_ost[0], 16 * bb)
                        s_oev.inc(nc.vector.scalar_tensor_tensor(
                            outb[:, d, :], PB[2 + d % 2][:, 0:NB], b2c[:, d:d + 1],
                            x1[:, bb % 2, d, :], ALU.add, ALU.add))
                    WG(nc.sync, s_oev, 8 * bb + 8)
                    s_ost[0].inc(nc.sync.dma_start(
                        outT_r[:, :, bb * NB:(bb + 1) * NB], outb[:, :, :]), 16)
                if b >= 1 and b + 1 < NBB:
                    t = b + 1
                    WG(nc.sync, s_lnx2, 8 * (t - 1))
                    WG(nc.sync, s_sq2p, 8 * (t - 1))
                    WG(nc.sync, s_cast2, 8 * (t - 1))
                    WG(nc.sync, s_oev, 8 * (t - 1))
                    s_xb[t % 2].inc(nc.sync.dma_start(
                        x1[:, t % 2, :, :], xT_r[:, :, t * NB:(t + 1) * NB]), 16)

            WG(nc.sync, s_ost[0], 16 * NBB)

        for s in allsems:
            assert s.maxwait <= s.n, f"wait {s.maxwait} > final {s.n} on {s.name}"
    return nc


_NC_CACHE = None


def kernel(L0_attn_x, L0_attn_w_qkv, L0_attn_w_proj, L0_attn_b_proj,
           L0_ffn_w1, L0_ffn_b1, L0_ffn_w2, L0_ffn_b2):
    global _NC_CACHE
    x = np.asarray(L0_attn_x, dtype=np.float32)
    w_qkv = np.asarray(L0_attn_w_qkv, dtype=np.float32)
    w_proj = np.asarray(L0_attn_w_proj, dtype=np.float32)
    b_proj = np.asarray(L0_attn_b_proj, dtype=np.float32)
    w1 = np.asarray(L0_ffn_w1, dtype=np.float32)
    b1 = np.asarray(L0_ffn_b1, dtype=np.float32)
    w2 = np.asarray(L0_ffn_w2, dtype=np.float32)
    b2 = np.asarray(L0_ffn_b2, dtype=np.float32)
    assert x.shape == (B, N, C) and w_qkv.shape == (3 * C, C)

    wqkvT = np.ascontiguousarray(w_qkv.T)
    wprojT = np.ascontiguousarray(w_proj.T)
    bprojc = np.ascontiguousarray(b_proj.reshape(CK, 128).T)
    b1c = np.ascontiguousarray(b1.reshape(WK, 128).T)
    b2c = np.ascontiguousarray(b2.reshape(CK, 128).T)
    w2r = np.ascontiguousarray(
        w2.reshape(WK, 128, CK, 128).transpose(2, 1, 0, 3)).reshape(CK * 128, WK, 128)
    zsel = np.zeros((16, CK, 128), dtype=np.float32)
    for p in range(CK):
        zsel[2 * p, p, 0:64] = 1.0
        zsel[2 * p + 1, p, 64:128] = 1.0

    if _NC_CACHE is None:
        _NC_CACHE = build_nc()
    nc = _NC_CACHE

    in_maps = []
    for bb in range(B):
        in_maps.append({
            "xT": np.ascontiguousarray(x[bb].T),
            "wqkvT": wqkvT, "wprojT": wprojT, "bprojc": bprojc,
            "w1": w1, "b1c": b1c, "w2r": w2r, "b2c": b2c, "zsel": zsel,
        })
    res = run_bass_kernel_spmd(nc, in_maps, core_ids=list(range(B)))
    out = np.stack([np.ascontiguousarray(r["outT"].T) for r in res.results], axis=0)
    return out


# version 27
# speedup vs baseline: 1.2351x; 1.2351x over previous
import numpy as np
from contextlib import ExitStack

import concourse.bass as bass
import concourse.mybir as mybir
from concourse.bass_utils import run_bass_kernel_spmd

F32 = mybir.dt.float32
BF16 = mybir.dt.bfloat16
AF = mybir.ActivationFunctionType
ALU = mybir.AluOpType

B = 8
N = 4096
C = 1024
CK = C // 128
W = 4096
WK = W // 128
NB_A = 512
NBA = N // NB_A
NB_B = 256
NBB = N // NB_B
NSUB = NB_A // 128
EPS_LN = 1e-5
EPS_Z = float(np.finfo(np.float32).eps)
NW2BUF = 4


class Sem:
    def __init__(self, stack, nc, name):
        self.h = stack.enter_context(nc.semaphore(name))
        self.name = name
        self.n = 0
        self.maxwait = 0

    def inc(self, inst, k=1):
        inst.then_inc(self.h, k)
        self.n += k
        return self.n


def WG(eng, sem, v):
    if v > 0:
        sem.maxwait = max(sem.maxwait, v)
        eng.wait_ge(sem.h, v)


def build_nc():
    nc = bass.Bass("TRN2", target_bir_lowering=False)

    xT = nc.dram_tensor("xT", [C, N], F32, kind="ExternalInput")
    wqkvT = nc.dram_tensor("wqkvT", [C, 3 * C], F32, kind="ExternalInput")
    wprojT = nc.dram_tensor("wprojT", [C, C], F32, kind="ExternalInput")
    bprojc_h = nc.dram_tensor("bprojc", [128, CK], F32, kind="ExternalInput")
    w1_h = nc.dram_tensor("w1", [C, W], F32, kind="ExternalInput")
    b1c_h = nc.dram_tensor("b1c", [128, WK], F32, kind="ExternalInput")
    w2r_h = nc.dram_tensor("w2r", [CK * 128, WK, 128], F32, kind="ExternalInput")
    b2c_h = nc.dram_tensor("b2c", [128, CK], F32, kind="ExternalInput")
    zsel_h = nc.dram_tensor("zsel", [16, CK, 128], F32, kind="ExternalInput")
    outT = nc.dram_tensor("outT", [C, N], F32, kind="ExternalOutput")
    qsc = nc.dram_tensor("qsc", [C, N], BF16, kind="Internal")
    w2bf = nc.dram_tensor("w2bf", [CK * 128, WK, 128], BF16, kind="Internal")

    xT_r = xT.ap().rearrange("(c p) n -> p c n", p=128)
    wqkvT_r = wqkvT.ap().rearrange("(c p) j -> p c j", p=128)
    wprojT_r = wprojT.ap().rearrange("(c p) j -> p c j", p=128)
    w1_r = w1_h.ap().rearrange("(c p) w -> p c w", p=128)
    qsc_r = qsc.ap().rearrange("(c p) n -> p c n", p=128)
    outT_r = outT.ap().rearrange("(c p) n -> p c n", p=128)

    st = ExitStack()
    with st:
        S = lambda name: Sem(st, nc, name)
        SB = lambda name, shape, dt: st.enter_context(nc.sbuf_tensor(name, shape, dt))
        w1_sb = SB("w1_sb", [128, CK, W], BF16)
        wproj_sb = SB("wproj_sb", [128, CK, C], BF16)
        ksmat = SB("ksmat", [128, CK, 16], BF16)
        ctxt = SB("ctxt", [128, CK, 128], BF16)
        bprojc = SB("bprojc_sb", [128, CK], F32)
        b1c = SB("b1c_sb", [128, WK], F32)
        b2c = SB("b2c_sb", [128, CK], F32)
        ones128 = SB("ones128", [128, 1], BF16)
        ones1b = SB("ones1b", [1, 128], BF16)

        s_ones = S("s_ones")
        s_wq = S("s_wq"); s_w1 = S("s_w1"); s_wp = S("s_wp")
        s_w2pc = S("s_w2pc"); s_cst = S("s_cst")
        s_xa = [S("s_xa0"), S("s_xa1")]
        s_sq = S("s_sq"); s_mmst = S("s_mmst"); s_rowv = S("s_rowv")
        s_rstd = S("s_rstd"); s_rown = S("s_rown"); s_mmbc = S("s_mmbc")
        s_pa1 = S("s_pa1"); s_lnx = S("s_lnx")
        s_mmq = S("s_mmq"); s_qev = S("s_qev")
        s_qst = [S("s_qst0"), S("s_qst1")]
        s_mmkv = S("s_mmkv"); s_kev = S("s_kev"); s_vev = S("s_vev")
        s_mmctx = S("s_mmctx"); s_ctxa = S("s_ctxa"); s_tail = S("s_tail")
        s_xb = [S(f"s_xb{r}") for r in range(3)]
        s_qb = [S(f"s_qb{r}") for r in range(2)]
        s_w2 = [S(f"s_w2_{r}") for r in range(NW2BUF)]
        s_mmzd = S("s_mmzd"); s_z = S("s_z"); s_mmzbc = S("s_mmzbc")
        s_qz = S("s_qz"); s_mmattn = S("s_mmattn"); s_aev = S("s_aev")
        s_mmproj = S("s_mmproj"); s_x1 = S("s_x1"); s_pev = S("s_pev")
        s_cast2 = S("s_cast2"); s_sq2 = S("s_sq2"); s_mmst2 = S("s_mmst2")
        s_rowv2 = S("s_rowv2"); s_rstd2 = S("s_rstd2"); s_rown2 = S("s_rown2")
        s_mmbc2 = S("s_mmbc2"); s_pa2 = S("s_pa2"); s_lnx2 = S("s_lnx2")
        s_mmf1 = S("s_mmf1"); s_hev = S("s_hev")
        s_mmf2 = S("s_mmf2"); s_oev = S("s_oev")
        s_ost = [S("s_ost0"), S("s_ost1")]
        s_vch = S("s_vch"); s_sch = S("s_sch")
        s_sqp = S("s_sqp"); s_sq2p = S("s_sq2p"); s_mz2 = S("s_mz2")
        s_mz1 = S("s_mz1"); s_zl = S("s_zl")
        s_rhl = S("s_rhl"); s_rhl2 = S("s_rhl2")

        allsems = []
        for v in list(locals().values()):
            if isinstance(v, Sem):
                allsems.append(v)
            elif isinstance(v, list) and v and isinstance(v[0], Sem):
                allsems.extend(v)

        nc.vector.memset(ones128[:, :], 1.0)
        nc.vector.memset(ones1b[:, :], 1.0)
        s_cst.inc(nc.sync.dma_start(bprojc[:, :], bprojc_h.ap()), 16)
        s_cst.inc(nc.sync.dma_start(b1c[:, :], b1c_h.ap()), 16)
        s_cst.inc(nc.sync.dma_start(b2c[:, :], b2c_h.ap()), 16)

        with ExitStack() as sta:
            SA = lambda name, shape, dt: sta.enter_context(nc.sbuf_tensor(name, shape, dt))
            ones1f = SA("ones1f", [1, 128], F32)
            s_ones.inc(nc.vector.memset(ones1f[:, :], 1.0))
            wqkv_sb = SA("wqkv_sb", [128, CK, 3 * C], BF16)
            xa = SA("xa", [128, 2, CK, NB_A], BF16)
            lnx = SA("lnx", [128, 2, CK, NB_A], BF16)
            qout = SA("qout", [128, 2, NB_A], BF16)
            kvt = SA("kvt", [128, 4, 2056], BF16)
            ctx_acc = SA("ctxacc", [128, 1032], F32)
            tmp = SA("tmpA", [128, NB_A], F32)
            xsq = SA("xsqA", [128, CK, NB_A], BF16)
            mean_row = SA("meanA", [1, NB_A], F32)[0:1, :]
            msq_row = SA("msqA", [1, NB_A], F32)[0:1, :]
            m2_row = SA("m2A", [1, NB_A], F32)[0:1, :]
            var_row = SA("varA", [1, NB_A], F32)[0:1, :]
            lnv_row = msq_row
            rstd_row = m2_row
            negm_row = var_row
            rhiA = SA("rhiA", [1, NB_A], BF16)[0:1, :]
            rloA = SA("rloA", [1, NB_A], BF16)[0:1, :]
            nhiA = SA("nhiA", [1, NB_A], BF16)[0:1, :]
            nloA = SA("nloA", [1, NB_A], BF16)[0:1, :]

            PS = lambda name: sta.enter_context(nc.psum_tensor(name, [128, 512], F32))
            PAq = [PS("PAq0"), PS("PAq1")]
            PAkv = [PS("PAkv0"), PS("PAkv1")]
            PActx = [PS("PActx0"), PS("PActx1"), PS("PActx2")]
            PAst = PS("PAst")
            CTXB = [(0, 3), (3, 3), (6, 2)]

            nc.vector.memset(PActx[0][:, 129:387], 0.0)
            nc.vector.memset(PActx[1][:, 129:387], 0.0)
            s_ctxa.inc(nc.vector.memset(PActx[2][:, 129:258], 0.0))
            s_ones.inc(nc.vector.memset(
                kvt[:, :, 1024:2056].rearrange("p s (pr x) -> p s pr x", x=129)[:, :, :, 128:129],
                1.0))

            s_xa[0].inc(nc.gpsimd.dma_start(xa[:, 0, :, :], xT_r[:, :, 0:NB_A]), 16)
            s_wq.inc(nc.gpsimd.dma_start(wqkv_sb[:, :, :], wqkvT_r), 16)
            s_xa[1].inc(nc.gpsimd.dma_start(xa[:, 1, :, :], xT_r[:, :, NB_A:2 * NB_A]), 16)
            s_w1.inc(nc.gpsimd.dma_start(w1_sb[:, :, :], w1_r), 16)
            s_wp.inc(nc.gpsimd.dma_start(wproj_sb[:, :, :], wprojT_r), 16)
            for d in range(CK):
                s_w2pc.inc(nc.gpsimd.dma_start(
                    w2bf.ap()[d * 128:(d + 1) * 128, :, :],
                    w2r_h.ap()[d * 128:(d + 1) * 128, :, :]), 16)

            def emit_ctx_block(ii):
                WG(nc.tensor, s_kev, 8 * ii + 8)
                WG(nc.tensor, s_vev, 8 * ii + 8)
                WG(nc.tensor, s_ctxa, ii + 1)
                if ii == 0:
                    WG(nc.tensor, s_ones, 2)
                for q, (p0, np_) in enumerate(CTXB):
                    for pi in range(np_):
                        for s in range(NSUB):
                            g = 4 * ii + s
                            sl = g % 4
                            p = p0 + pi
                            ctx_mm = nc.tensor.matmul(
                                PActx[q][:, pi * 129:(pi + 1) * 129],
                                kvt[:, sl, p * 128:(p + 1) * 128],
                                kvt[:, sl, 1024 + p * 129:1024 + (p + 1) * 129],
                                start=(pi == 0 and s == 0), stop=(pi == np_ - 1 and s == NSUB - 1),
                                skip_group_check=True)
                s_mmctx.inc(ctx_mm)

            def emit_ctx_acc_block(ii):
                WG(nc.vector, s_mmctx, ii + 1)
                last = None
                for q, (p0, np_) in enumerate(CTXB):
                    w_ = np_ * 129
                    dst = ctx_acc[:, p0 * 129:p0 * 129 + w_]
                    if ii == 0:
                        last = nc.vector.tensor_copy(dst, PActx[q][:, 0:w_])
                    else:
                        last = nc.vector.tensor_tensor(
                            dst, dst, PActx[q][:, 0:w_], ALU.add)
                s_vch.inc(last)
                WG(nc.vector, s_vch, s_vch.n)
                nc.vector.memset(PActx[0][:, 129:387], 0.0)
                nc.vector.memset(PActx[1][:, 129:387], 0.0)
                s_ctxa.inc(nc.vector.memset(PActx[2][:, 129:258], 0.0))

            for i in range(NBA + 1):
                if i == 0:
                    pass
                elif i + 1 < NBA:
                    WG(nc.gpsimd, s_pa1, 8 * i)
                    WG(nc.gpsimd, s_sqp, 8 * i)

                    s_xa[(i + 1) % 2].inc(nc.gpsimd.dma_start(
                        xa[:, (i + 1) % 2, :, :], xT_r[:, :, (i + 1) * NB_A:(i + 2) * NB_A]), 16)

                def emit_squaresA(i2):
                    for c in range(CK):
                        WG(nc.vector, s_xa[i2 % 2], 16 * (i2 // 2 + 1))
                        WG(nc.vector, s_mmst, 8 * (i2 - 1) + c + 1)
                        s_sqp.inc(nc.vector.tensor_tensor(
                            xsq[:, c, :], xa[:, i2 % 2, c, :], xa[:, i2 % 2, c, :], ALU.mult))

                def emit_statsA(i2):
                    for c in range(CK):
                        if c == 0:
                            WG(nc.tensor, s_ones, 1)
                            WG(nc.tensor, s_rowv, i2)
                            WG(nc.tensor, s_mz1, i2 + 1)
                            WG(nc.tensor, s_xa[i2 % 2], 16 * (i2 // 2 + 1))
                        nc.tensor.matmul(PAst[0:1, 0:NB_A], ones128[:, :], xa[:, i2 % 2, c, :],
                                         start=(c == 0), stop=False,
                                         skip_group_check=True)
                    for c in range(CK):
                        WG(nc.tensor, s_sqp, 8 * i2 + c + 1)
                        s_mmst.inc(nc.tensor.matmul(PAst[32:33, 0:NB_A], ones128[:, :], xsq[:, c, :],
                                                    start=False, stop=(c == CK - 1),
                                                    tile_position=(0, 32),
                                                    skip_group_check=True))

                if i == 0:
                    s_mz1.inc(nc.scalar.memzero(PAst[32:33, 0:NB_A]))
                    emit_squaresA(0)
                    emit_statsA(0)

                if i < NBA:
                    WG(nc.vector, s_mmst, 8 * (i + 1))
                    nc.vector.tensor_scalar(mean_row, PAst[0:1, 0:NB_A], 1.0 / C, None, ALU.mult)
                    s_vch.inc(nc.vector.tensor_scalar(msq_row, PAst[32:33, 0:NB_A], 1.0 / C, None, ALU.mult))
                    WG(nc.vector, s_vch, s_vch.n)
                    s_vch.inc(nc.vector.tensor_tensor(m2_row, mean_row, mean_row, ALU.mult))
                    WG(nc.vector, s_vch, s_vch.n)
                    s_rowv.inc(nc.vector.scalar_tensor_tensor(
                        var_row, msq_row, EPS_LN, m2_row, ALU.add, ALU.subtract))
                    WG(nc.scalar, s_rowv, i + 1)
                    s_sch.inc(nc.scalar.activation(lnv_row, var_row, AF.Ln))
                    WG(nc.scalar, s_sch, s_sch.n)
                    s_rstd.inc(nc.scalar.activation(rstd_row, lnv_row, AF.Exp, scale=-0.5))
                    if i + 1 <= NBA - 1:
                        s_mz1.inc(nc.scalar.memzero(PAst[32:33, 0:NB_A]))
                    WG(nc.vector, s_rstd, i + 1)
                    s_rown.inc(nc.vector.scalar_tensor_tensor(
                        negm_row, mean_row, -1.0, rstd_row, ALU.mult, ALU.mult))
                if i >= 1:
                    ii = i - 1
                    for j in range(CK):
                        if j == 0:
                            WG(nc.tensor, s_lnx, 8 * ii + 8)
                            if ii == 0:
                                WG(nc.tensor, s_wq, 16)
                        elif j >= 2:
                            WG(nc.tensor, s_qev, 8 * ii + j - 1)
                        for c in range(CK):
                            q_mm = nc.tensor.matmul(PAq[j % 2][:, 0:NB_A],
                                                    wqkv_sb[:, c, j * 128:(j + 1) * 128],
                                                    lnx[:, ii % 2, c, :],
                                                    start=(c == 0), stop=(c == CK - 1))
                        s_mmq.inc(q_mm)
                        WG(nc.scalar, s_mmq, 8 * ii + j + 1)
                        kq = 8 * ii + j
                        WG(nc.scalar, s_qst[j % 2], 16 * (kq // 2))
                        s_qev.inc(nc.scalar.activation(qout[:, j % 2, :], PAq[j % 2][:, 0:NB_A], AF.Relu))
                        WG(nc.sync, s_qev, 8 * ii + j + 1)
                        s_qst[j % 2].inc(nc.sync.dma_start(
                            qsc_r[:, j, ii * NB_A:(ii + 1) * NB_A], qout[:, j % 2, :]), 16)

                if i < NBA:
                    WG(nc.tensor, s_rstd, i + 1)
                    WG(nc.tensor, s_qev, 8 * (i - 1) + 7)
                    s_mmbc.inc(nc.tensor.matmul(PAq[0][:, 0:NB_A], ones1f[:, :], rstd_row,
                                                start=True, stop=True))
                    WG(nc.tensor, s_rown, i + 1)
                    WG(nc.tensor, s_qev, 8 * (i - 1) + 8)
                    s_mmbc.inc(nc.tensor.matmul(PAq[1][:, 0:NB_A], ones1f[:, :], negm_row,
                                                start=True, stop=True))

                if i < NBA:
                    for c in range(CK):
                        if c == 0:
                            WG(nc.vector, s_mmbc, 2 * i + 2)
                            WG(nc.vector, s_mmkv, 16 * (i - 1))
                        else:
                            WG(nc.vector, s_lnx, 8 * i + c)
                        s_pa1.inc(nc.vector.tensor_tensor(
                            tmp[:, :], xa[:, i % 2, c, :], PAq[0][:, 0:NB_A], ALU.mult))
                        WG(nc.vector, s_pa1, 8 * i + c + 1)
                        s_lnx.inc(nc.vector.tensor_tensor(
                            lnx[:, i % 2, c, :], tmp[:, :], PAq[1][:, 0:NB_A], ALU.add))

                if 1 <= i + 1 <= NBA - 1:
                    emit_squaresA(i + 1)

                if i >= 1:
                    ii = i - 1
                    for s in range(NSUB):
                        g = 4 * ii + s
                        sl = g % 4
                        for idx, jt in enumerate([0, 1, 2, 3]):
                            bank = idx % 2
                            if idx == 0:
                                WG(nc.tensor, s_vev, 2 * g - 1)
                            elif idx == 1:
                                WG(nc.tensor, s_vev, 2 * g)
                            elif idx == 2:
                                WG(nc.tensor, s_kev, 2 * g + 1)
                            else:
                                WG(nc.tensor, s_kev, 2 * g + 2)
                            for c in range(CK):
                                kv_mm = nc.tensor.matmul(
                                    PAkv[bank][:, 0:NB_A],
                                    lnx[:, ii % 2, c, s * 128:(s + 1) * 128],
                                    wqkv_sb[:, c, C + jt * 512:C + (jt + 1) * 512],
                                    start=(c == 0), stop=(c == CK - 1))
                            s_mmkv.inc(kv_mm)
                            WG(nc.scalar, s_mmkv, 4 * g + idx + 1)
                            WG(nc.scalar, s_mmctx, ii)
                            if jt < 2:
                                s_kev.inc(nc.scalar.activation(
                                    kvt[:, sl, jt * 512:(jt + 1) * 512],
                                    PAkv[bank][:, 0:NB_A], AF.Relu))
                            else:
                                vv = kvt[:, sl, 1024:2056].rearrange(
                                    "p (pr x) -> p pr x", x=129)[:, (jt - 2) * 4:(jt - 1) * 4, 0:128]
                                s_vev.inc(nc.scalar.activation(
                                    vv, PAkv[bank][:, 0:NB_A].rearrange("p (a b) -> p a b", b=128),
                                    AF.Copy))
                    emit_ctx_block(ii)
                    emit_ctx_acc_block(ii)

                if i + 1 <= NBA - 1 and i >= 0:
                    emit_statsA(i + 1)


            nc.gpsimd.memset(ksmat[:, :, :], 0.0)
            s_tail.inc(nc.gpsimd.memset(ctxt[:, :, :], 0.0))
            WG(nc.vector, s_tail, 1)
            WG(nc.vector, s_ctxa, NBA + 1)
            for p in range(CK):
                base = p * 129
                nc.vector.tensor_copy(ksmat[0:64, p, 2 * p:2 * p + 1],
                                      ctx_acc[0:64, base + 128:base + 129])
                nc.vector.tensor_copy(ksmat[64:128, p, 2 * p + 1:2 * p + 2],
                                      ctx_acc[64:128, base + 128:base + 129])
                nc.vector.tensor_copy(ctxt[0:64, p, 0:64],
                                      ctx_acc[0:64, base:base + 64])
                nc.vector.tensor_copy(ctxt[64:128, p, 64:128],
                                      ctx_acc[64:128, base + 64:base + 128])
            nc.all_engine_barrier()

        with ExitStack() as stb:
            SBB = lambda name, shape, dt: stb.enter_context(nc.sbuf_tensor(name, shape, dt))
            projtmp = SBB("projtmp", [128, CK, NB_B], F32)
            qb = SBB("qb", [128, 2, CK, NB_B], BF16)
            x1 = SBB("x1", [128, 2, CK, NB_B], F32)
            zbcs = SBB("zbcs", [128, 2, NB_B], F32)
            ln2 = SBB("ln2", [128, 2, CK, NB_B], BF16)
            attnT = SBB("attnT", [128, CK, NB_B], BF16)
            hT = SBB("hT", [128, WK, NB_B], BF16)
            outb = SBB("outb", [128, CK, NB_B], F32)
            w2t = SBB("w2t", [128, NW2BUF, WK, 128], BF16)
            tmp2 = SBB("tmp2", [128, NB_B], F32)
            xsq2 = SBB("xsq2", [128, CK, NB_B], BF16)
            xbf2 = SBB("xbf2", [128, CK, NB_B], BF16)
            ztmp = SBB("ztmp", [16, NB_B], F32)
            zrow = SBB("zrow", [16, NB_B], F32)
            mean2 = SBB("mean2", [1, NB_B], F32)
            msq2 = SBB("msq2", [1, NB_B], F32)
            m22 = SBB("m22", [1, NB_B], F32)
            var2 = SBB("var2", [1, NB_B], F32)
            lnv2 = msq2
            rstd2 = m22
            negm2 = var2
            rhiB = SBB("rhiB", [1, NB_B], BF16)[0:1, :]
            rloB = SBB("rloB", [1, NB_B], BF16)[0:1, :]
            nhiB = SBB("nhiB", [1, NB_B], BF16)[0:1, :]
            nloB = SBB("nloB", [1, NB_B], BF16)[0:1, :]
            zsel_sb = SBB("zsel_sb", [16, CK, 128], BF16)
            zhi = SBB("zhi", [16, NB_B], BF16)
            zlo = SBB("zlo", [16, NB_B], BF16)
            s_zl.inc(nc.gpsimd.dma_start(zsel_sb[:, :, :], zsel_h.ap()), 16)

            PS2 = lambda name: stb.enter_context(nc.psum_tensor(name, [128, 512], F32))
            PB = [PS2(f"PB{k}") for k in range(8)]
            NB = NB_B

            def emit_rowpre2(bb):
                WG(nc.vector, s_mmst2, 8 * (bb + 1))
                nc.vector.tensor_scalar(mean2[:, :], PB[7][0:1, 0:NB], 1.0 / C, None, ALU.mult)
                s_vch.inc(nc.vector.tensor_scalar(msq2[:, :], PB[7][32:33, 0:NB], 1.0 / C, None, ALU.mult))
                WG(nc.vector, s_vch, s_vch.n)
                s_vch.inc(nc.vector.tensor_tensor(m22[:, :], mean2[:, :], mean2[:, :], ALU.mult))
                WG(nc.vector, s_vch, s_vch.n)
                s_rowv2.inc(nc.vector.scalar_tensor_tensor(
                    var2[:, :], msq2[:, :], EPS_LN, m22[:, :], ALU.add, ALU.subtract))

            def emit_rowpre2_act(bb):
                WG(nc.scalar, s_rowv2, bb + 1)
                s_sch.inc(nc.scalar.activation(lnv2[:, :], var2[:, :], AF.Ln))
                WG(nc.scalar, s_sch, s_sch.n)
                s_rstd2.inc(nc.scalar.activation(rstd2[:, :], lnv2[:, :], AF.Exp, scale=-0.5))

            def emit_negm2(bb):
                WG(nc.vector, s_rstd2, bb + 1)
                s_vch.inc(nc.vector.scalar_tensor_tensor(
                    negm2[:, :], mean2[:, :], -1.0, rstd2[:, :], ALU.mult, ALU.mult))
                WG(nc.vector, s_vch, s_vch.n)
                s_vch.inc(nc.vector.tensor_copy(rhiB, rstd2[:, :]))
                WG(nc.vector, s_vch, s_vch.n)
                s_vch.inc(nc.vector.tensor_tensor(rloB, rstd2[:, :], rhiB, ALU.subtract))
                WG(nc.vector, s_vch, s_vch.n)
                s_vch.inc(nc.vector.tensor_copy(nhiB, negm2[:, :]))
                WG(nc.vector, s_vch, s_vch.n)
                s_rhl2.inc(nc.vector.tensor_tensor(nloB, negm2[:, :], nhiB, ALU.subtract))

            def emit_zchain(b2):
                WG(nc.vector, s_mmzd, b2 + 1)
                s_vch.inc(nc.vector.tensor_scalar(ztmp[:, :], PB[0][0:16, 0:NB], EPS_Z, None, ALU.add))
                WG(nc.vector, s_vch, s_vch.n)
                s_vch.inc(nc.vector.reciprocal(zrow[:, :], ztmp[:, :]))
                WG(nc.vector, s_vch, s_vch.n)
                s_vch.inc(nc.vector.tensor_copy(zhi[:, :], zrow[:, :]))
                WG(nc.vector, s_vch, s_vch.n)
                s_z.inc(nc.vector.tensor_tensor(zlo[:, :], zrow[:, :], zhi[:, :], ALU.subtract))

            def emit_zden(b2):
                for c in range(CK):
                    if c == 0:
                        WG(nc.tensor, s_qb[b2 % 2], 16 * (b2 // 2 + 1))
                        WG(nc.tensor, s_hev, 32 * (b2 - 2) + 31)
                        WG(nc.tensor, s_aev, 8 * (b2 - 1) + 8)
                    zd_mm = nc.tensor.matmul(PB[0][0:16, 0:NB], ksmat[:, c, :], qb[:, b2 % 2, c, :],
                                             start=(c == 0), stop=(c == CK - 1))
                s_mmzd.inc(zd_mm)


            for b in range(NBB + 1):
                if b == 0:
                    s_mz2.inc(nc.scalar.memzero(PB[7][32:33, 0:NB]))
                    WG(nc.sync, s_qst[0], 16 * (4 * NBA))
                    WG(nc.sync, s_qst[1], 16 * (4 * NBA))
                    for t in range(2):
                        s_qb[t].inc(nc.sync.dma_start(
                            qb[:, t, :, :], qsc_r[:, :, t * NB:(t + 1) * NB]), 16)
                    for t in range(2):
                        s_xb[t].inc(nc.sync.dma_start(
                            x1[:, t, :, :], xT_r[:, :, t * NB:(t + 1) * NB]), 16)
                else:
                    if b + 1 < NBB:
                        t = b + 1
                        WG(nc.sync, s_mmattn, 8 * b)
                        WG(nc.sync, s_qst[0], 16 * 4 * (t // 2 + 1))
                        WG(nc.sync, s_qst[1], 16 * 4 * (t // 2 + 1))
                        s_qb[t % 2].inc(nc.sync.dma_start(
                            qb[:, t % 2, :, :], qsc_r[:, :, t * NB:(t + 1) * NB]), 16)
                def emit_w2loads(bb, parity):
                    for d in range(parity, CK, 2):
                        eng = nc.sync if d % 2 == 0 else nc.gpsimd
                        if bb == 0 and d <= 1:
                            WG(eng, s_w2pc, 16 * CK)
                        WG(eng, s_mmf2, 8 * bb + d - 3)
                        s_w2[d % 4].inc(eng.dma_start(
                            w2t[:, d % 4, :, :],
                            w2bf.ap()[d * 128:(d + 1) * 128, :, :]), 16)
                if b >= 1:
                    emit_w2loads(b - 1, 0)

                def emit_squaresB(b2):
                    for c in range(CK):
                        WG(nc.gpsimd, s_x1, 8 * (b2 + 1))
                        WG(nc.gpsimd, s_mmst2, 8 * (b2 - 1) + c + 1)
                        s_cast2.inc(nc.gpsimd.tensor_copy(xbf2[:, c, :], x1[:, b2 % 2, c, :]))
                        WG(nc.vector, s_x1, 8 * (b2 + 1))
                        WG(nc.vector, s_mmst2, 8 * (b2 - 1) + c + 1)
                        s_sq2p.inc(nc.vector.tensor_tensor(
                            xsq2[:, c, :], x1[:, b2 % 2, c, :], x1[:, b2 % 2, c, :], ALU.mult))

                def emit_statsB(b2):
                    for c in range(CK):
                        if c == 0:
                            WG(nc.tensor, s_rowv2, b2)
                            WG(nc.tensor, s_x1, 8 * (b2 + 1))
                            WG(nc.tensor, s_mz2, b2 + 1)
                        WG(nc.tensor, s_cast2, 8 * b2 + c + 1)
                        nc.tensor.matmul(PB[7][0:1, 0:NB], ones128[:, :], xbf2[:, c, :],
                                         start=(c == 0), stop=False,
                                         skip_group_check=True)
                    for c in range(CK):
                        WG(nc.tensor, s_sq2p, 8 * b2 + c + 1)
                        s_mmst2.inc(nc.tensor.matmul(PB[7][32:33, 0:NB], ones128[:, :], xsq2[:, c, :],
                                                     start=False, stop=(c == CK - 1),
                                                     tile_position=(0, 32),
                                                     skip_group_check=True))

                if b < NBB:
                    if b == 0:
                        emit_zden(0)
                        emit_zchain(0)
                    ABANK = (4, 0, 1)
                    ZBANK = (5, 2, 3)

                    def emit_attn(p):
                        pa = ABANK[p % 3]
                        if p == 0:
                            WG(nc.tensor, s_qb[b % 2], 16 * (b // 2 + 1))
                            WG(nc.tensor, s_lnx2, 8 * b)
                        elif p == 1:
                            WG(nc.tensor, s_hev, 32 * (b - 2) + 31)
                        elif p == 2:
                            WG(nc.tensor, s_hev, 32 * (b - 2) + 32)
                        else:
                            WG(nc.tensor, s_aev, 8 * b + p - 2)
                        s_mmattn.inc(nc.tensor.matmul(PB[pa][:, 0:NB], ctxt[:, p, :],
                                                      qb[:, b % 2, p, :], start=True, stop=True))

                    def emit_zbc(p):
                        pz = ZBANK[p % 3]
                        if p == 0:
                            WG(nc.tensor, s_z, b + 1)
                            WG(nc.tensor, s_lnx2, 8 * b)
                            if b == 0:
                                WG(nc.tensor, s_zl, 16)
                        elif p == 1:
                            WG(nc.tensor, s_oev, 8 * (b - 2) + 7)
                        elif p == 2:
                            WG(nc.tensor, s_oev, 8 * (b - 2) + 8)
                        else:
                            WG(nc.tensor, s_qz, 8 * b + p - 2)
                        nc.tensor.matmul(PB[pz][:, 0:NB], zsel_sb[:, p, :],
                                         zhi[:, :], start=True, stop=False)
                        s_mmzbc.inc(nc.tensor.matmul(PB[pz][:, 0:NB], zsel_sb[:, p, :],
                                                     zlo[:, :], start=False, stop=True))

                    for p in range(CK):
                        emit_zbc(p)
                        emit_attn(p)
                    for p in range(CK):
                        if p <= 1:
                            WG(nc.scalar, s_aev, 8 * b + p - 1)
                        else:
                            WG(nc.scalar, s_aev, 8 * b + p - 1)
                        WG(nc.scalar, s_mmzbc, 8 * b + p + 1)
                        s_qz.inc(nc.scalar.activation(zbcs[:, p % 2, :],
                                                      PB[ZBANK[p % 3]][:, 0:NB], AF.Copy))
                        WG(nc.vector, s_mmattn, 8 * b + p + 1)
                        WG(nc.vector, s_qz, 8 * b + p + 1)
                        if p == 0:
                            WG(nc.vector, s_mmproj, 8 * b)
                        s_aev.inc(nc.vector.tensor_tensor(
                            attnT[:, p, :], PB[ABANK[p % 3]][:, 0:NB], zbcs[:, p % 2, :], ALU.mult))

                def emit_bcast2(bb):
                    WG(nc.tensor, s_rhl2, bb + 1)
                    WG(nc.tensor, s_lnx2, 8 * bb)
                    WG(nc.tensor, s_aev, 8 * bb + 7)
                    WG(nc.tensor, s_qz, 8 * bb + 7)
                    nc.tensor.matmul(PB[4][:, 0:NB], ones1b[:, :], rhiB,
                                     start=True, stop=False)
                    s_mmbc2.inc(nc.tensor.matmul(PB[4][:, 0:NB], ones1b[:, :], rloB,
                                                 start=False, stop=True))
                    nc.tensor.matmul(PB[5][:, 0:NB], ones1b[:, :], nhiB,
                                     start=True, stop=False)
                    s_mmbc2.inc(nc.tensor.matmul(PB[5][:, 0:NB], ones1b[:, :], nloB,
                                                 start=False, stop=True))

                if b < NBB:
                    for j in range(CK):
                        if j == 0:
                            WG(nc.tensor, s_pev, 8 * (b - 1) + 7)
                            if b == 0:
                                WG(nc.tensor, s_wp, 16)
                        elif j == 1:
                            WG(nc.tensor, s_rowv2, b)
                            WG(nc.tensor, s_mz2, b + 1)
                            WG(nc.tensor, s_pev, 8 * (b - 1) + 8)
                        else:
                            WG(nc.tensor, s_pev, 8 * b + j - 1)
                        for d in range(CK):
                            if j == 0:
                                WG(nc.tensor, s_aev, 8 * b + d + 1)
                            pj_mm = nc.tensor.matmul(PB[6 + j % 2][:, 0:NB],
                                                     wproj_sb[:, d, j * 128:(j + 1) * 128],
                                                     attnT[:, d, :],
                                                     start=(d == 0), stop=(d == CK - 1))
                        s_mmproj.inc(pj_mm)
                        WG(nc.scalar, s_mmproj, 8 * b + j + 1)
                        WG(nc.scalar, s_x1, 8 * (b - 1) + j + 1)
                        s_pev.inc(nc.scalar.activation(projtmp[:, j, :],
                                                       PB[6 + j % 2][:, 0:NB], AF.Copy))
                def emit_pass2(bb, k):
                    if k == 0:
                        WG(nc.vector, s_mmbc2, 2 * bb + 2)
                        WG(nc.vector, s_mmf1, 32 * (bb - 1))
                    else:
                        WG(nc.vector, s_lnx2, 8 * bb + k)
                    s_pa2.inc(nc.vector.tensor_tensor(
                        tmp2[:, :], x1[:, bb % 2, k, :], PB[4][:, 0:NB], ALU.mult))
                    WG(nc.vector, s_pa2, 8 * bb + k + 1)
                    s_lnx2.inc(nc.vector.tensor_tensor(
                        ln2[:, bb % 2, k, :], tmp2[:, :], PB[5][:, 0:NB], ALU.add))

                def emit_x1(b2, k):
                    WG(nc.vector, s_pev, 8 * b2 + k + 1)
                    if k == 0:
                        WG(nc.vector, s_xb[b2 % 2], 16 * (b2 // 2 + 1))
                        if b2 == 0:
                            WG(nc.vector, s_cst, 48)
                    s_x1.inc(nc.vector.scalar_tensor_tensor(
                        x1[:, b2 % 2, k, :], projtmp[:, k, :], bprojc[:, k:k + 1],
                        x1[:, b2 % 2, k, :], ALU.add, ALU.add))

                if b < NBB:
                    for k in range(CK):
                        emit_x1(b, k)
                if b == 0:
                    emit_zden(1)
                    emit_zchain(1)
                if b < NBB:
                    emit_squaresB(b)
                if b >= 1:
                    emit_w2loads(b - 1, 1)

                def emit_rows(b2):
                    emit_rowpre2(b2)
                    emit_rowpre2_act(b2)
                    if b2 + 1 < NBB:
                        s_mz2.inc(nc.scalar.memzero(PB[7][32:33, 0:NB]))
                    emit_negm2(b2)

                if b == 0:
                    emit_statsB(0)
                    emit_rows(0)
                    emit_bcast2(0)
                    for k in range(CK):
                        emit_pass2(0, k)

                if b >= 1:
                    bb = b - 1
                    for w in range(WK):
                        if w == 0:
                            WG(nc.tensor, s_aev, 8 * b + 8 if b < NBB else s_aev.n)
                            if bb == 0:
                                WG(nc.tensor, s_w1, 16)
                        elif w == 1:
                            WG(nc.tensor, s_aev, 8 * b + 6 if b < NBB else s_aev.n)
                        else:
                            WG(nc.tensor, s_hev, 32 * bb + w - 1)
                        for c in range(CK):
                            if w == 0:
                                WG(nc.tensor, s_lnx2, 8 * bb + c + 1)
                            f1_mm = nc.tensor.matmul(PB[w % 2][:, 0:NB],
                                                     w1_sb[:, c, w * 128:(w + 1) * 128],
                                                     ln2[:, bb % 2, c, :],
                                                     start=(c == 0), stop=(c == CK - 1))
                        s_mmf1.inc(f1_mm)
                        WG(nc.scalar, s_mmf1, 32 * bb + w + 1)
                        if w == 0:
                            WG(nc.scalar, s_mmf2, 8 * bb)
                            if bb == 0:
                                WG(nc.scalar, s_cst, 48)
                        s_hev.inc(nc.scalar.activation(hT[:, w, :], PB[w % 2][:, 0:NB],
                                                       AF.Relu, bias=b1c[:, w:w + 1]))
                        if b < NBB:
                            if w == 11:
                                emit_statsB(b)
                            elif w == 13:
                                emit_rows(b)
                            elif w == 19:
                                emit_bcast2(b)
                            elif w == 21:
                                for k in range(CK):
                                    emit_pass2(b, k)
                    if b + 1 < NBB:
                        emit_zden(b + 1)
                        emit_zchain(b + 1)
                    for d in range(CK):
                        if d == 0:
                            WG(nc.tensor, s_hev, 32 * bb + 32)
                            WG(nc.tensor, s_qz, 8 * b + 8 if b < NBB else s_qz.n)
                        elif d == 1 and b < NBB:
                            WG(nc.tensor, s_qz, 8 * b + 6)
                        WG(nc.tensor, s_w2[d % 4], 16 * (2 * bb + d // 4 + 1))
                        WG(nc.tensor, s_oev, 8 * bb + d - 1)
                        for w in range(WK):
                            f2_mm = nc.tensor.matmul(PB[2 + d % 2][:, 0:NB],
                                                     w2t[:, d % 4, w, :],
                                                     hT[:, w, :],
                                                     start=(w == 0), stop=(w == WK - 1))
                        s_mmf2.inc(f2_mm)
                        WG(nc.vector, s_mmf2, 8 * bb + d + 1)
                        if d == 0:
                            WG(nc.vector, s_ost[0], 16 * bb)
                        s_oev.inc(nc.vector.scalar_tensor_tensor(
                            outb[:, d, :], PB[2 + d % 2][:, 0:NB], b2c[:, d:d + 1],
                            x1[:, bb % 2, d, :], ALU.add, ALU.add))
                    WG(nc.sync, s_oev, 8 * bb + 8)
                    s_ost[0].inc(nc.sync.dma_start(
                        outT_r[:, :, bb * NB:(bb + 1) * NB], outb[:, :, :]), 16)
                if b >= 1 and b + 1 < NBB:
                    t = b + 1
                    WG(nc.sync, s_lnx2, 8 * (t - 1))
                    WG(nc.sync, s_sq2p, 8 * (t - 1))
                    WG(nc.sync, s_cast2, 8 * (t - 1))
                    WG(nc.sync, s_oev, 8 * (t - 1))
                    s_xb[t % 2].inc(nc.sync.dma_start(
                        x1[:, t % 2, :, :], xT_r[:, :, t * NB:(t + 1) * NB]), 16)

            WG(nc.sync, s_ost[0], 16 * NBB)

        for s in allsems:
            assert s.maxwait <= s.n, f"wait {s.maxwait} > final {s.n} on {s.name}"
    return nc


_NC_CACHE = None


def kernel(L0_attn_x, L0_attn_w_qkv, L0_attn_w_proj, L0_attn_b_proj,
           L0_ffn_w1, L0_ffn_b1, L0_ffn_w2, L0_ffn_b2):
    global _NC_CACHE
    x = np.asarray(L0_attn_x, dtype=np.float32)
    w_qkv = np.asarray(L0_attn_w_qkv, dtype=np.float32)
    w_proj = np.asarray(L0_attn_w_proj, dtype=np.float32)
    b_proj = np.asarray(L0_attn_b_proj, dtype=np.float32)
    w1 = np.asarray(L0_ffn_w1, dtype=np.float32)
    b1 = np.asarray(L0_ffn_b1, dtype=np.float32)
    w2 = np.asarray(L0_ffn_w2, dtype=np.float32)
    b2 = np.asarray(L0_ffn_b2, dtype=np.float32)
    assert x.shape == (B, N, C) and w_qkv.shape == (3 * C, C)

    wqkvT = np.ascontiguousarray(w_qkv.T)
    wprojT = np.ascontiguousarray(w_proj.T)
    bprojc = np.ascontiguousarray(b_proj.reshape(CK, 128).T)
    b1c = np.ascontiguousarray(b1.reshape(WK, 128).T)
    b2c = np.ascontiguousarray(b2.reshape(CK, 128).T)
    w2r = np.ascontiguousarray(
        w2.reshape(WK, 128, CK, 128).transpose(2, 1, 0, 3)).reshape(CK * 128, WK, 128)
    zsel = np.zeros((16, CK, 128), dtype=np.float32)
    for p in range(CK):
        zsel[2 * p, p, 0:64] = 1.0
        zsel[2 * p + 1, p, 64:128] = 1.0

    if _NC_CACHE is None:
        _NC_CACHE = build_nc()
    nc = _NC_CACHE

    in_maps = []
    for bb in range(B):
        in_maps.append({
            "xT": np.ascontiguousarray(x[bb].T),
            "wqkvT": wqkvT, "wprojT": wprojT, "bprojc": bprojc,
            "w1": w1, "b1c": b1c, "w2r": w2r, "b2c": b2c, "zsel": zsel,
        })
    res = run_bass_kernel_spmd(nc, in_maps, core_ids=list(range(B)))
    out = np.stack([np.ascontiguousarray(r["outT"].T) for r in res.results], axis=0)
    return out


# version 28
# speedup vs baseline: 1.2432x; 1.0065x over previous
import numpy as np
from contextlib import ExitStack

import concourse.bass as bass
import concourse.mybir as mybir
from concourse.bass_utils import run_bass_kernel_spmd

F32 = mybir.dt.float32
BF16 = mybir.dt.bfloat16
AF = mybir.ActivationFunctionType
ALU = mybir.AluOpType

B = 8
N = 4096
C = 1024
CK = C // 128
W = 4096
WK = W // 128
NB_A = 512
NBA = N // NB_A
NB_B = 256
NBB = N // NB_B
NSUB = NB_A // 128
EPS_LN = 1e-5
EPS_Z = float(np.finfo(np.float32).eps)
NW2BUF = 4


class Sem:
    def __init__(self, stack, nc, name):
        self.h = stack.enter_context(nc.semaphore(name))
        self.name = name
        self.n = 0
        self.maxwait = 0

    def inc(self, inst, k=1):
        inst.then_inc(self.h, k)
        self.n += k
        return self.n


def WG(eng, sem, v):
    if v > 0:
        sem.maxwait = max(sem.maxwait, v)
        eng.wait_ge(sem.h, v)


def build_nc():
    nc = bass.Bass("TRN2", target_bir_lowering=False)

    xT = nc.dram_tensor("xT", [C, N], F32, kind="ExternalInput")
    wqkvT = nc.dram_tensor("wqkvT", [C, 3 * C], F32, kind="ExternalInput")
    wprojT = nc.dram_tensor("wprojT", [C, C], F32, kind="ExternalInput")
    bprojc_h = nc.dram_tensor("bprojc", [128, CK], F32, kind="ExternalInput")
    w1_h = nc.dram_tensor("w1", [C, W], F32, kind="ExternalInput")
    b1c_h = nc.dram_tensor("b1c", [128, WK], F32, kind="ExternalInput")
    w2r_h = nc.dram_tensor("w2r", [CK * 128, WK, 128], F32, kind="ExternalInput")
    b2c_h = nc.dram_tensor("b2c", [128, CK], F32, kind="ExternalInput")
    zsel_h = nc.dram_tensor("zsel", [16, CK, 128], F32, kind="ExternalInput")
    outT = nc.dram_tensor("outT", [C, N], F32, kind="ExternalOutput")
    qsc = nc.dram_tensor("qsc", [C, N], BF16, kind="Internal")
    w2bf = nc.dram_tensor("w2bf", [CK * 128, WK, 128], BF16, kind="Internal")

    xT_r = xT.ap().rearrange("(c p) n -> p c n", p=128)
    wqkvT_r = wqkvT.ap().rearrange("(c p) j -> p c j", p=128)
    wprojT_r = wprojT.ap().rearrange("(c p) j -> p c j", p=128)
    w1_r = w1_h.ap().rearrange("(c p) w -> p c w", p=128)
    qsc_r = qsc.ap().rearrange("(c p) n -> p c n", p=128)
    outT_r = outT.ap().rearrange("(c p) n -> p c n", p=128)

    st = ExitStack()
    with st:
        S = lambda name: Sem(st, nc, name)
        SB = lambda name, shape, dt: st.enter_context(nc.sbuf_tensor(name, shape, dt))
        w1_sb = SB("w1_sb", [128, CK, W], BF16)
        wproj_sb = SB("wproj_sb", [128, CK, C], BF16)
        ksmat = SB("ksmat", [128, CK, 16], BF16)
        ctxt = SB("ctxt", [128, CK, 128], BF16)
        bprojc = SB("bprojc_sb", [128, CK], F32)
        b1c = SB("b1c_sb", [128, WK], F32)
        b2c = SB("b2c_sb", [128, CK], F32)
        ones128 = SB("ones128", [128, 1], BF16)
        ones1b = SB("ones1b", [1, 128], BF16)

        s_ones = S("s_ones")
        s_wq = S("s_wq"); s_w1 = S("s_w1"); s_wp = S("s_wp")
        s_w2pc = S("s_w2pc"); s_cst = S("s_cst")
        s_xa = [S("s_xa0"), S("s_xa1")]
        s_sq = S("s_sq"); s_mmst = S("s_mmst"); s_rowv = S("s_rowv")
        s_rstd = S("s_rstd"); s_rown = S("s_rown"); s_mmbc = S("s_mmbc")
        s_pa1 = S("s_pa1"); s_lnx = S("s_lnx")
        s_mmq = S("s_mmq"); s_qev = S("s_qev")
        s_qst = [S("s_qst0"), S("s_qst1")]
        s_mmkv = S("s_mmkv"); s_kev = S("s_kev"); s_vev = S("s_vev")
        s_mmctx = S("s_mmctx"); s_ctxa = S("s_ctxa"); s_tail = S("s_tail")
        s_xb = [S(f"s_xb{r}") for r in range(3)]
        s_qb = [S(f"s_qb{r}") for r in range(2)]
        s_w2 = [S(f"s_w2_{r}") for r in range(NW2BUF)]
        s_mmzd = S("s_mmzd"); s_z = S("s_z"); s_mmzbc = S("s_mmzbc")
        s_qz = S("s_qz"); s_mmattn = S("s_mmattn"); s_aev = S("s_aev")
        s_mmproj = S("s_mmproj"); s_x1 = S("s_x1"); s_pev = S("s_pev")
        s_cast2 = S("s_cast2"); s_sq2 = S("s_sq2"); s_mmst2 = S("s_mmst2")
        s_rowv2 = S("s_rowv2"); s_rstd2 = S("s_rstd2"); s_rown2 = S("s_rown2")
        s_mmbc2 = S("s_mmbc2"); s_pa2 = S("s_pa2"); s_lnx2 = S("s_lnx2")
        s_mmf1 = S("s_mmf1"); s_hev = S("s_hev")
        s_mmf2 = S("s_mmf2"); s_oev = S("s_oev")
        s_ost = [S("s_ost0"), S("s_ost1")]
        s_vch = S("s_vch"); s_sch = S("s_sch")
        s_sqp = S("s_sqp"); s_sq2p = S("s_sq2p"); s_mz2 = S("s_mz2")
        s_mz1 = S("s_mz1"); s_zl = S("s_zl")
        s_rhl = S("s_rhl"); s_rhl2 = S("s_rhl2")

        allsems = []
        for v in list(locals().values()):
            if isinstance(v, Sem):
                allsems.append(v)
            elif isinstance(v, list) and v and isinstance(v[0], Sem):
                allsems.extend(v)

        nc.vector.memset(ones128[:, :], 1.0)
        nc.vector.memset(ones1b[:, :], 1.0)
        s_cst.inc(nc.sync.dma_start(bprojc[:, :], bprojc_h.ap()), 16)
        s_cst.inc(nc.sync.dma_start(b1c[:, :], b1c_h.ap()), 16)
        s_cst.inc(nc.sync.dma_start(b2c[:, :], b2c_h.ap()), 16)

        with ExitStack() as sta:
            SA = lambda name, shape, dt: sta.enter_context(nc.sbuf_tensor(name, shape, dt))
            ones1f = SA("ones1f", [1, 128], F32)
            s_ones.inc(nc.vector.memset(ones1f[:, :], 1.0))
            wqkv_sb = SA("wqkv_sb", [128, CK, 3 * C], BF16)
            xa = SA("xa", [128, 2, CK, NB_A], BF16)
            lnx = SA("lnx", [128, 2, CK, NB_A], BF16)
            qout = SA("qout", [128, 2, NB_A], BF16)
            kvt = SA("kvt", [128, 4, 2056], BF16)
            ctx_acc = SA("ctxacc", [128, 1032], F32)
            tmp = SA("tmpA", [128, NB_A], F32)
            xsq = SA("xsqA", [128, CK, NB_A], BF16)
            mean_row = SA("meanA", [1, NB_A], F32)[0:1, :]
            msq_row = SA("msqA", [1, NB_A], F32)[0:1, :]
            m2_row = SA("m2A", [1, NB_A], F32)[0:1, :]
            var_row = SA("varA", [1, NB_A], F32)[0:1, :]
            lnv_row = msq_row
            rstd_row = m2_row
            negm_row = var_row
            rhiA = SA("rhiA", [1, NB_A], BF16)[0:1, :]
            rloA = SA("rloA", [1, NB_A], BF16)[0:1, :]
            nhiA = SA("nhiA", [1, NB_A], BF16)[0:1, :]
            nloA = SA("nloA", [1, NB_A], BF16)[0:1, :]

            PS = lambda name: sta.enter_context(nc.psum_tensor(name, [128, 512], F32))
            PAq = [PS("PAq0"), PS("PAq1")]
            PAkv = [PS("PAkv0"), PS("PAkv1")]
            PActx = [PS("PActx0"), PS("PActx1"), PS("PActx2")]
            PAst = PS("PAst")
            CTXB = [(0, 3), (3, 3), (6, 2)]

            nc.vector.memset(PActx[0][:, 129:387], 0.0)
            nc.vector.memset(PActx[1][:, 129:387], 0.0)
            s_ctxa.inc(nc.vector.memset(PActx[2][:, 129:258], 0.0))
            s_ones.inc(nc.vector.memset(
                kvt[:, :, 1024:2056].rearrange("p s (pr x) -> p s pr x", x=129)[:, :, :, 128:129],
                1.0))

            s_xa[0].inc(nc.gpsimd.dma_start(xa[:, 0, :, :], xT_r[:, :, 0:NB_A]), 16)
            s_wq.inc(nc.gpsimd.dma_start(wqkv_sb[:, :, :], wqkvT_r), 16)
            s_xa[1].inc(nc.gpsimd.dma_start(xa[:, 1, :, :], xT_r[:, :, NB_A:2 * NB_A]), 16)

            def emit_wloads(i):
                if i == 1:
                    s_w1.inc(nc.gpsimd.dma_start(w1_sb[:, :, :], w1_r), 16)
                elif i == 2:
                    s_wp.inc(nc.gpsimd.dma_start(wproj_sb[:, :, :], wprojT_r), 16)
                elif 3 <= i <= 6:
                    for d in range(2 * (i - 3), 2 * (i - 2)):
                        s_w2pc.inc(nc.gpsimd.dma_start(
                            w2bf.ap()[d * 128:(d + 1) * 128, :, :],
                            w2r_h.ap()[d * 128:(d + 1) * 128, :, :]), 16)

            def emit_ctx_block(ii):
                WG(nc.tensor, s_kev, 8 * ii + 8)
                WG(nc.tensor, s_vev, 8 * ii + 8)
                WG(nc.tensor, s_ctxa, ii + 1)
                if ii == 0:
                    WG(nc.tensor, s_ones, 2)
                for q, (p0, np_) in enumerate(CTXB):
                    for pi in range(np_):
                        for s in range(NSUB):
                            g = 4 * ii + s
                            sl = g % 4
                            p = p0 + pi
                            ctx_mm = nc.tensor.matmul(
                                PActx[q][:, pi * 129:(pi + 1) * 129],
                                kvt[:, sl, p * 128:(p + 1) * 128],
                                kvt[:, sl, 1024 + p * 129:1024 + (p + 1) * 129],
                                start=(pi == 0 and s == 0), stop=(pi == np_ - 1 and s == NSUB - 1),
                                skip_group_check=True)
                s_mmctx.inc(ctx_mm)

            def emit_ctx_acc_block(ii):
                WG(nc.vector, s_mmctx, ii + 1)
                last = None
                for q, (p0, np_) in enumerate(CTXB):
                    w_ = np_ * 129
                    dst = ctx_acc[:, p0 * 129:p0 * 129 + w_]
                    if ii == 0:
                        last = nc.vector.tensor_copy(dst, PActx[q][:, 0:w_])
                    else:
                        last = nc.vector.tensor_tensor(
                            dst, dst, PActx[q][:, 0:w_], ALU.add)
                s_vch.inc(last)
                WG(nc.vector, s_vch, s_vch.n)
                nc.vector.memset(PActx[0][:, 129:387], 0.0)
                nc.vector.memset(PActx[1][:, 129:387], 0.0)
                s_ctxa.inc(nc.vector.memset(PActx[2][:, 129:258], 0.0))

            for i in range(NBA + 1):
                if i == 0:
                    pass
                elif i + 1 < NBA:
                    WG(nc.gpsimd, s_pa1, 8 * i)
                    WG(nc.gpsimd, s_sqp, 8 * i)

                    s_xa[(i + 1) % 2].inc(nc.gpsimd.dma_start(
                        xa[:, (i + 1) % 2, :, :], xT_r[:, :, (i + 1) * NB_A:(i + 2) * NB_A]), 16)
                if i >= 1:
                    emit_wloads(i)

                def emit_squaresA(i2):
                    for c in range(CK):
                        WG(nc.vector, s_xa[i2 % 2], 16 * (i2 // 2 + 1))
                        WG(nc.vector, s_mmst, 8 * (i2 - 1) + c + 1)
                        s_sqp.inc(nc.vector.tensor_tensor(
                            xsq[:, c, :], xa[:, i2 % 2, c, :], xa[:, i2 % 2, c, :], ALU.mult))

                def emit_statsA(i2):
                    for c in range(CK):
                        if c == 0:
                            WG(nc.tensor, s_ones, 1)
                            WG(nc.tensor, s_rowv, i2)
                            WG(nc.tensor, s_mz1, i2 + 1)
                            WG(nc.tensor, s_xa[i2 % 2], 16 * (i2 // 2 + 1))
                        nc.tensor.matmul(PAst[0:1, 0:NB_A], ones128[:, :], xa[:, i2 % 2, c, :],
                                         start=(c == 0), stop=False,
                                         skip_group_check=True)
                    for c in range(CK):
                        WG(nc.tensor, s_sqp, 8 * i2 + c + 1)
                        s_mmst.inc(nc.tensor.matmul(PAst[32:33, 0:NB_A], ones128[:, :], xsq[:, c, :],
                                                    start=False, stop=(c == CK - 1),
                                                    tile_position=(0, 32),
                                                    skip_group_check=True))

                if i == 0:
                    s_mz1.inc(nc.scalar.memzero(PAst[32:33, 0:NB_A]))
                    emit_squaresA(0)
                    emit_statsA(0)

                if i < NBA:
                    WG(nc.vector, s_mmst, 8 * (i + 1))
                    nc.vector.tensor_scalar(mean_row, PAst[0:1, 0:NB_A], 1.0 / C, None, ALU.mult)
                    s_vch.inc(nc.vector.tensor_scalar(msq_row, PAst[32:33, 0:NB_A], 1.0 / C, None, ALU.mult))
                    WG(nc.vector, s_vch, s_vch.n)
                    s_vch.inc(nc.vector.tensor_tensor(m2_row, mean_row, mean_row, ALU.mult))
                    WG(nc.vector, s_vch, s_vch.n)
                    s_rowv.inc(nc.vector.scalar_tensor_tensor(
                        var_row, msq_row, EPS_LN, m2_row, ALU.add, ALU.subtract))
                    WG(nc.scalar, s_rowv, i + 1)
                    s_sch.inc(nc.scalar.activation(lnv_row, var_row, AF.Ln))
                    WG(nc.scalar, s_sch, s_sch.n)
                    s_rstd.inc(nc.scalar.activation(rstd_row, lnv_row, AF.Exp, scale=-0.5))
                    if i + 1 <= NBA - 1:
                        s_mz1.inc(nc.scalar.memzero(PAst[32:33, 0:NB_A]))
                    WG(nc.vector, s_rstd, i + 1)
                    s_rown.inc(nc.vector.scalar_tensor_tensor(
                        negm_row, mean_row, -1.0, rstd_row, ALU.mult, ALU.mult))
                if i >= 1:
                    ii = i - 1
                    for j in range(CK):
                        if j == 0:
                            WG(nc.tensor, s_lnx, 8 * ii + 8)
                            if ii == 0:
                                WG(nc.tensor, s_wq, 16)
                        elif j >= 2:
                            WG(nc.tensor, s_qev, 8 * ii + j - 1)
                        for c in range(CK):
                            q_mm = nc.tensor.matmul(PAq[j % 2][:, 0:NB_A],
                                                    wqkv_sb[:, c, j * 128:(j + 1) * 128],
                                                    lnx[:, ii % 2, c, :],
                                                    start=(c == 0), stop=(c == CK - 1))
                        s_mmq.inc(q_mm)
                        WG(nc.scalar, s_mmq, 8 * ii + j + 1)
                        kq = 8 * ii + j
                        WG(nc.scalar, s_qst[j % 2], 16 * (kq // 2))
                        s_qev.inc(nc.scalar.activation(qout[:, j % 2, :], PAq[j % 2][:, 0:NB_A], AF.Relu))
                        WG(nc.sync, s_qev, 8 * ii + j + 1)
                        s_qst[j % 2].inc(nc.sync.dma_start(
                            qsc_r[:, j, ii * NB_A:(ii + 1) * NB_A], qout[:, j % 2, :]), 16)

                if i < NBA:
                    WG(nc.tensor, s_rstd, i + 1)
                    WG(nc.tensor, s_qev, 8 * (i - 1) + 7)
                    s_mmbc.inc(nc.tensor.matmul(PAq[0][:, 0:NB_A], ones1f[:, :], rstd_row,
                                                start=True, stop=True))
                    WG(nc.tensor, s_rown, i + 1)
                    WG(nc.tensor, s_qev, 8 * (i - 1) + 8)
                    s_mmbc.inc(nc.tensor.matmul(PAq[1][:, 0:NB_A], ones1f[:, :], negm_row,
                                                start=True, stop=True))

                if i < NBA:
                    for c in range(CK):
                        if c == 0:
                            WG(nc.vector, s_mmbc, 2 * i + 2)
                            WG(nc.vector, s_mmkv, 16 * (i - 1))
                        else:
                            WG(nc.vector, s_lnx, 8 * i + c)
                        s_pa1.inc(nc.vector.tensor_tensor(
                            tmp[:, :], xa[:, i % 2, c, :], PAq[0][:, 0:NB_A], ALU.mult))
                        WG(nc.vector, s_pa1, 8 * i + c + 1)
                        s_lnx.inc(nc.vector.tensor_tensor(
                            lnx[:, i % 2, c, :], tmp[:, :], PAq[1][:, 0:NB_A], ALU.add))

                if 1 <= i + 1 <= NBA - 1:
                    emit_squaresA(i + 1)

                if i >= 1:
                    ii = i - 1
                    for s in range(NSUB):
                        g = 4 * ii + s
                        sl = g % 4
                        for idx, jt in enumerate([0, 1, 2, 3]):
                            bank = idx % 2
                            if idx == 0:
                                WG(nc.tensor, s_vev, 2 * g - 1)
                            elif idx == 1:
                                WG(nc.tensor, s_vev, 2 * g)
                            elif idx == 2:
                                WG(nc.tensor, s_kev, 2 * g + 1)
                            else:
                                WG(nc.tensor, s_kev, 2 * g + 2)
                            for c in range(CK):
                                kv_mm = nc.tensor.matmul(
                                    PAkv[bank][:, 0:NB_A],
                                    lnx[:, ii % 2, c, s * 128:(s + 1) * 128],
                                    wqkv_sb[:, c, C + jt * 512:C + (jt + 1) * 512],
                                    start=(c == 0), stop=(c == CK - 1))
                            s_mmkv.inc(kv_mm)
                            WG(nc.scalar, s_mmkv, 4 * g + idx + 1)
                            WG(nc.scalar, s_mmctx, ii)
                            if jt < 2:
                                s_kev.inc(nc.scalar.activation(
                                    kvt[:, sl, jt * 512:(jt + 1) * 512],
                                    PAkv[bank][:, 0:NB_A], AF.Relu))
                            else:
                                vv = kvt[:, sl, 1024:2056].rearrange(
                                    "p (pr x) -> p pr x", x=129)[:, (jt - 2) * 4:(jt - 1) * 4, 0:128]
                                s_vev.inc(nc.scalar.activation(
                                    vv, PAkv[bank][:, 0:NB_A].rearrange("p (a b) -> p a b", b=128),
                                    AF.Copy))
                    emit_ctx_block(ii)
                    emit_ctx_acc_block(ii)

                if i + 1 <= NBA - 1 and i >= 0:
                    emit_statsA(i + 1)


            nc.gpsimd.memset(ksmat[:, :, :], 0.0)
            s_tail.inc(nc.gpsimd.memset(ctxt[:, :, :], 0.0))
            WG(nc.vector, s_tail, 1)
            WG(nc.vector, s_ctxa, NBA + 1)
            for p in range(CK):
                base = p * 129
                nc.vector.tensor_copy(ksmat[0:64, p, 2 * p:2 * p + 1],
                                      ctx_acc[0:64, base + 128:base + 129])
                nc.vector.tensor_copy(ksmat[64:128, p, 2 * p + 1:2 * p + 2],
                                      ctx_acc[64:128, base + 128:base + 129])
                nc.vector.tensor_copy(ctxt[0:64, p, 0:64],
                                      ctx_acc[0:64, base:base + 64])
                nc.vector.tensor_copy(ctxt[64:128, p, 64:128],
                                      ctx_acc[64:128, base + 64:base + 128])
            nc.all_engine_barrier()

        with ExitStack() as stb:
            SBB = lambda name, shape, dt: stb.enter_context(nc.sbuf_tensor(name, shape, dt))
            projtmp = SBB("projtmp", [128, CK, NB_B], F32)
            qb = SBB("qb", [128, 2, CK, NB_B], BF16)
            x1 = SBB("x1", [128, 2, CK, NB_B], F32)
            zbcs = SBB("zbcs", [128, 2, NB_B], F32)
            ln2 = SBB("ln2", [128, 2, CK, NB_B], BF16)
            attnT = SBB("attnT", [128, CK, NB_B], BF16)
            hT = SBB("hT", [128, WK, NB_B], BF16)
            outb = SBB("outb", [128, CK, NB_B], F32)
            w2t = SBB("w2t", [128, NW2BUF, WK, 128], BF16)
            tmp2 = SBB("tmp2", [128, NB_B], F32)
            xsq2 = SBB("xsq2", [128, CK, NB_B], BF16)
            xbf2 = SBB("xbf2", [128, CK, NB_B], BF16)
            ztmp = SBB("ztmp", [16, NB_B], F32)
            zrow = SBB("zrow", [16, NB_B], F32)
            mean2 = SBB("mean2", [1, NB_B], F32)
            msq2 = SBB("msq2", [1, NB_B], F32)
            m22 = SBB("m22", [1, NB_B], F32)
            var2 = SBB("var2", [1, NB_B], F32)
            lnv2 = msq2
            rstd2 = m22
            negm2 = var2
            rhiB = SBB("rhiB", [1, NB_B], BF16)[0:1, :]
            rloB = SBB("rloB", [1, NB_B], BF16)[0:1, :]
            nhiB = SBB("nhiB", [1, NB_B], BF16)[0:1, :]
            nloB = SBB("nloB", [1, NB_B], BF16)[0:1, :]
            zsel_sb = SBB("zsel_sb", [16, CK, 128], BF16)
            zhi = SBB("zhi", [16, NB_B], BF16)
            zlo = SBB("zlo", [16, NB_B], BF16)
            s_zl.inc(nc.gpsimd.dma_start(zsel_sb[:, :, :], zsel_h.ap()), 16)

            PS2 = lambda name: stb.enter_context(nc.psum_tensor(name, [128, 512], F32))
            PB = [PS2(f"PB{k}") for k in range(8)]
            NB = NB_B

            def emit_rowpre2(bb):
                WG(nc.vector, s_mmst2, 8 * (bb + 1))
                nc.vector.tensor_scalar(mean2[:, :], PB[7][0:1, 0:NB], 1.0 / C, None, ALU.mult)
                s_vch.inc(nc.vector.tensor_scalar(msq2[:, :], PB[7][32:33, 0:NB], 1.0 / C, None, ALU.mult))
                WG(nc.vector, s_vch, s_vch.n)
                s_vch.inc(nc.vector.tensor_tensor(m22[:, :], mean2[:, :], mean2[:, :], ALU.mult))
                WG(nc.vector, s_vch, s_vch.n)
                s_rowv2.inc(nc.vector.scalar_tensor_tensor(
                    var2[:, :], msq2[:, :], EPS_LN, m22[:, :], ALU.add, ALU.subtract))

            def emit_rowpre2_act(bb):
                WG(nc.scalar, s_rowv2, bb + 1)
                s_sch.inc(nc.scalar.activation(lnv2[:, :], var2[:, :], AF.Ln))
                WG(nc.scalar, s_sch, s_sch.n)
                s_rstd2.inc(nc.scalar.activation(rstd2[:, :], lnv2[:, :], AF.Exp, scale=-0.5))

            def emit_negm2(bb):
                WG(nc.vector, s_rstd2, bb + 1)
                s_vch.inc(nc.vector.scalar_tensor_tensor(
                    negm2[:, :], mean2[:, :], -1.0, rstd2[:, :], ALU.mult, ALU.mult))
                WG(nc.vector, s_vch, s_vch.n)
                s_vch.inc(nc.vector.tensor_copy(rhiB, rstd2[:, :]))
                WG(nc.vector, s_vch, s_vch.n)
                s_vch.inc(nc.vector.tensor_tensor(rloB, rstd2[:, :], rhiB, ALU.subtract))
                WG(nc.vector, s_vch, s_vch.n)
                s_vch.inc(nc.vector.tensor_copy(nhiB, negm2[:, :]))
                WG(nc.vector, s_vch, s_vch.n)
                s_rhl2.inc(nc.vector.tensor_tensor(nloB, negm2[:, :], nhiB, ALU.subtract))

            def emit_zchain(b2):
                WG(nc.vector, s_mmzd, b2 + 1)
                s_vch.inc(nc.vector.tensor_scalar(ztmp[:, :], PB[0][0:16, 0:NB], EPS_Z, None, ALU.add))
                WG(nc.vector, s_vch, s_vch.n)
                s_vch.inc(nc.vector.reciprocal(zrow[:, :], ztmp[:, :]))
                WG(nc.vector, s_vch, s_vch.n)
                s_vch.inc(nc.vector.tensor_copy(zhi[:, :], zrow[:, :]))
                WG(nc.vector, s_vch, s_vch.n)
                s_z.inc(nc.vector.tensor_tensor(zlo[:, :], zrow[:, :], zhi[:, :], ALU.subtract))

            def emit_zden(b2):
                for c in range(CK):
                    if c == 0:
                        WG(nc.tensor, s_qb[b2 % 2], 16 * (b2 // 2 + 1))
                        WG(nc.tensor, s_hev, 32 * (b2 - 2) + 31)
                        WG(nc.tensor, s_aev, 8 * (b2 - 1) + 8)
                    zd_mm = nc.tensor.matmul(PB[0][0:16, 0:NB], ksmat[:, c, :], qb[:, b2 % 2, c, :],
                                             start=(c == 0), stop=(c == CK - 1))
                s_mmzd.inc(zd_mm)


            for b in range(NBB + 1):
                if b == 0:
                    s_mz2.inc(nc.scalar.memzero(PB[7][32:33, 0:NB]))
                    WG(nc.sync, s_qst[0], 16 * (4 * NBA))
                    WG(nc.sync, s_qst[1], 16 * (4 * NBA))
                    for t in range(2):
                        s_qb[t].inc(nc.sync.dma_start(
                            qb[:, t, :, :], qsc_r[:, :, t * NB:(t + 1) * NB]), 16)
                    for t in range(2):
                        s_xb[t].inc(nc.sync.dma_start(
                            x1[:, t, :, :], xT_r[:, :, t * NB:(t + 1) * NB]), 16)
                else:
                    if b + 1 < NBB:
                        t = b + 1
                        WG(nc.sync, s_mmattn, 8 * b)
                        WG(nc.sync, s_qst[0], 16 * 4 * (t // 2 + 1))
                        WG(nc.sync, s_qst[1], 16 * 4 * (t // 2 + 1))
                        s_qb[t % 2].inc(nc.sync.dma_start(
                            qb[:, t % 2, :, :], qsc_r[:, :, t * NB:(t + 1) * NB]), 16)
                def emit_w2loads(bb, parity):
                    for d in range(parity, CK, 2):
                        eng = nc.sync if d % 2 == 0 else nc.gpsimd
                        if bb == 0 and d <= 1:
                            WG(eng, s_w2pc, 16 * CK)
                        WG(eng, s_mmf2, 8 * bb + d - 3)
                        s_w2[d % 4].inc(eng.dma_start(
                            w2t[:, d % 4, :, :],
                            w2bf.ap()[d * 128:(d + 1) * 128, :, :]), 16)
                if b >= 1:
                    emit_w2loads(b - 1, 0)

                def emit_squaresB(b2):
                    for c in range(CK):
                        WG(nc.gpsimd, s_x1, 8 * (b2 + 1))
                        WG(nc.gpsimd, s_mmst2, 8 * (b2 - 1) + c + 1)
                        s_cast2.inc(nc.gpsimd.tensor_copy(xbf2[:, c, :], x1[:, b2 % 2, c, :]))
                        WG(nc.vector, s_x1, 8 * (b2 + 1))
                        WG(nc.vector, s_mmst2, 8 * (b2 - 1) + c + 1)
                        s_sq2p.inc(nc.vector.tensor_tensor(
                            xsq2[:, c, :], x1[:, b2 % 2, c, :], x1[:, b2 % 2, c, :], ALU.mult))

                def emit_statsB(b2):
                    for c in range(CK):
                        if c == 0:
                            WG(nc.tensor, s_rowv2, b2)
                            WG(nc.tensor, s_x1, 8 * (b2 + 1))
                            WG(nc.tensor, s_mz2, b2 + 1)
                        WG(nc.tensor, s_cast2, 8 * b2 + c + 1)
                        nc.tensor.matmul(PB[7][0:1, 0:NB], ones128[:, :], xbf2[:, c, :],
                                         start=(c == 0), stop=False,
                                         skip_group_check=True)
                    for c in range(CK):
                        WG(nc.tensor, s_sq2p, 8 * b2 + c + 1)
                        s_mmst2.inc(nc.tensor.matmul(PB[7][32:33, 0:NB], ones128[:, :], xsq2[:, c, :],
                                                     start=False, stop=(c == CK - 1),
                                                     tile_position=(0, 32),
                                                     skip_group_check=True))

                if b < NBB:
                    if b == 0:
                        emit_zden(0)
                        emit_zchain(0)
                    ABANK = (4, 0, 1)
                    ZBANK = (5, 2, 3)

                    def emit_attn(p):
                        pa = ABANK[p % 3]
                        if p == 0:
                            WG(nc.tensor, s_qb[b % 2], 16 * (b // 2 + 1))
                            WG(nc.tensor, s_lnx2, 8 * b)
                        elif p == 1:
                            WG(nc.tensor, s_hev, 32 * (b - 2) + 31)
                        elif p == 2:
                            WG(nc.tensor, s_hev, 32 * (b - 2) + 32)
                        else:
                            WG(nc.tensor, s_aev, 8 * b + p - 2)
                        s_mmattn.inc(nc.tensor.matmul(PB[pa][:, 0:NB], ctxt[:, p, :],
                                                      qb[:, b % 2, p, :], start=True, stop=True))

                    def emit_zbc(p):
                        pz = ZBANK[p % 3]
                        if p == 0:
                            WG(nc.tensor, s_z, b + 1)
                            WG(nc.tensor, s_lnx2, 8 * b)
                            if b == 0:
                                WG(nc.tensor, s_zl, 16)
                        elif p == 1:
                            WG(nc.tensor, s_oev, 8 * (b - 2) + 7)
                        elif p == 2:
                            WG(nc.tensor, s_oev, 8 * (b - 2) + 8)
                        else:
                            WG(nc.tensor, s_qz, 8 * b + p - 2)
                        nc.tensor.matmul(PB[pz][:, 0:NB], zsel_sb[:, p, :],
                                         zhi[:, :], start=True, stop=False)
                        s_mmzbc.inc(nc.tensor.matmul(PB[pz][:, 0:NB], zsel_sb[:, p, :],
                                                     zlo[:, :], start=False, stop=True))

                    for p in range(CK):
                        emit_zbc(p)
                        emit_attn(p)
                    for p in range(CK):
                        if p <= 1:
                            WG(nc.scalar, s_aev, 8 * b + p - 1)
                        else:
                            WG(nc.scalar, s_aev, 8 * b + p - 1)
                        WG(nc.scalar, s_mmzbc, 8 * b + p + 1)
                        s_qz.inc(nc.scalar.activation(zbcs[:, p % 2, :],
                                                      PB[ZBANK[p % 3]][:, 0:NB], AF.Copy))
                        WG(nc.vector, s_mmattn, 8 * b + p + 1)
                        WG(nc.vector, s_qz, 8 * b + p + 1)
                        if p == 0:
                            WG(nc.vector, s_mmproj, 8 * b)
                        s_aev.inc(nc.vector.tensor_tensor(
                            attnT[:, p, :], PB[ABANK[p % 3]][:, 0:NB], zbcs[:, p % 2, :], ALU.mult))

                def emit_bcast2(bb):
                    WG(nc.tensor, s_rhl2, bb + 1)
                    WG(nc.tensor, s_lnx2, 8 * bb)
                    WG(nc.tensor, s_aev, 8 * bb + 7)
                    WG(nc.tensor, s_qz, 8 * bb + 7)
                    nc.tensor.matmul(PB[4][:, 0:NB], ones1b[:, :], rhiB,
                                     start=True, stop=False)
                    s_mmbc2.inc(nc.tensor.matmul(PB[4][:, 0:NB], ones1b[:, :], rloB,
                                                 start=False, stop=True))
                    nc.tensor.matmul(PB[5][:, 0:NB], ones1b[:, :], nhiB,
                                     start=True, stop=False)
                    s_mmbc2.inc(nc.tensor.matmul(PB[5][:, 0:NB], ones1b[:, :], nloB,
                                                 start=False, stop=True))

                if b < NBB:
                    for j in range(CK):
                        if j == 0:
                            WG(nc.tensor, s_pev, 8 * (b - 1) + 7)
                            if b == 0:
                                WG(nc.tensor, s_wp, 16)
                        elif j == 1:
                            WG(nc.tensor, s_rowv2, b)
                            WG(nc.tensor, s_mz2, b + 1)
                            WG(nc.tensor, s_pev, 8 * (b - 1) + 8)
                        else:
                            WG(nc.tensor, s_pev, 8 * b + j - 1)
                        for d in range(CK):
                            if j == 0:
                                WG(nc.tensor, s_aev, 8 * b + d + 1)
                            pj_mm = nc.tensor.matmul(PB[6 + j % 2][:, 0:NB],
                                                     wproj_sb[:, d, j * 128:(j + 1) * 128],
                                                     attnT[:, d, :],
                                                     start=(d == 0), stop=(d == CK - 1))
                        s_mmproj.inc(pj_mm)
                        WG(nc.scalar, s_mmproj, 8 * b + j + 1)
                        WG(nc.scalar, s_x1, 8 * (b - 1) + j + 1)
                        s_pev.inc(nc.scalar.activation(projtmp[:, j, :],
                                                       PB[6 + j % 2][:, 0:NB], AF.Copy))
                def emit_pass2(bb, k):
                    if k == 0:
                        WG(nc.vector, s_mmbc2, 2 * bb + 2)
                        WG(nc.vector, s_mmf1, 32 * (bb - 1))
                    else:
                        WG(nc.vector, s_lnx2, 8 * bb + k)
                    s_pa2.inc(nc.vector.tensor_tensor(
                        tmp2[:, :], x1[:, bb % 2, k, :], PB[4][:, 0:NB], ALU.mult))
                    WG(nc.vector, s_pa2, 8 * bb + k + 1)
                    s_lnx2.inc(nc.vector.tensor_tensor(
                        ln2[:, bb % 2, k, :], tmp2[:, :], PB[5][:, 0:NB], ALU.add))

                def emit_x1(b2, k):
                    WG(nc.vector, s_pev, 8 * b2 + k + 1)
                    if k == 0:
                        WG(nc.vector, s_xb[b2 % 2], 16 * (b2 // 2 + 1))
                        if b2 == 0:
                            WG(nc.vector, s_cst, 48)
                    s_x1.inc(nc.vector.scalar_tensor_tensor(
                        x1[:, b2 % 2, k, :], projtmp[:, k, :], bprojc[:, k:k + 1],
                        x1[:, b2 % 2, k, :], ALU.add, ALU.add))

                if b < NBB:
                    for k in range(CK):
                        emit_x1(b, k)
                if b == 0:
                    emit_zden(1)
                    emit_zchain(1)
                if b < NBB:
                    emit_squaresB(b)
                if b >= 1:
                    emit_w2loads(b - 1, 1)

                def emit_rows(b2):
                    emit_rowpre2(b2)
                    emit_rowpre2_act(b2)
                    if b2 + 1 < NBB:
                        s_mz2.inc(nc.scalar.memzero(PB[7][32:33, 0:NB]))
                    emit_negm2(b2)

                if b == 0:
                    emit_statsB(0)
                    emit_rows(0)
                    emit_bcast2(0)
                    for k in range(CK):
                        emit_pass2(0, k)

                if b >= 1:
                    bb = b - 1
                    for w in range(WK):
                        if w == 0:
                            WG(nc.tensor, s_aev, 8 * b + 8 if b < NBB else s_aev.n)
                            if bb == 0:
                                WG(nc.tensor, s_w1, 16)
                        elif w == 1:
                            WG(nc.tensor, s_aev, 8 * b + 6 if b < NBB else s_aev.n)
                        else:
                            WG(nc.tensor, s_hev, 32 * bb + w - 1)
                        for c in range(CK):
                            if w == 0:
                                WG(nc.tensor, s_lnx2, 8 * bb + c + 1)
                            f1_mm = nc.tensor.matmul(PB[w % 2][:, 0:NB],
                                                     w1_sb[:, c, w * 128:(w + 1) * 128],
                                                     ln2[:, bb % 2, c, :],
                                                     start=(c == 0), stop=(c == CK - 1))
                        s_mmf1.inc(f1_mm)
                        WG(nc.scalar, s_mmf1, 32 * bb + w + 1)
                        if w == 0:
                            WG(nc.scalar, s_mmf2, 8 * bb)
                            if bb == 0:
                                WG(nc.scalar, s_cst, 48)
                        s_hev.inc(nc.scalar.activation(hT[:, w, :], PB[w % 2][:, 0:NB],
                                                       AF.Relu, bias=b1c[:, w:w + 1]))
                        if b < NBB:
                            if w == 11:
                                emit_statsB(b)
                            elif w == 13:
                                emit_rows(b)
                            elif w == 19:
                                emit_bcast2(b)
                            elif w == 21:
                                for k in range(CK):
                                    emit_pass2(b, k)
                    if b + 1 < NBB:
                        emit_zden(b + 1)
                        emit_zchain(b + 1)
                    for d in range(CK):
                        if d == 0:
                            WG(nc.tensor, s_hev, 32 * bb + 32)
                            WG(nc.tensor, s_qz, 8 * b + 8 if b < NBB else s_qz.n)
                        elif d == 1 and b < NBB:
                            WG(nc.tensor, s_qz, 8 * b + 6)
                        WG(nc.tensor, s_w2[d % 4], 16 * (2 * bb + d // 4 + 1))
                        WG(nc.tensor, s_oev, 8 * bb + d - 1)
                        for w in range(WK):
                            f2_mm = nc.tensor.matmul(PB[2 + d % 2][:, 0:NB],
                                                     w2t[:, d % 4, w, :],
                                                     hT[:, w, :],
                                                     start=(w == 0), stop=(w == WK - 1))
                        s_mmf2.inc(f2_mm)
                        WG(nc.vector, s_mmf2, 8 * bb + d + 1)
                        if d == 0:
                            WG(nc.vector, s_ost[0], 16 * bb)
                        s_oev.inc(nc.vector.scalar_tensor_tensor(
                            outb[:, d, :], PB[2 + d % 2][:, 0:NB], b2c[:, d:d + 1],
                            x1[:, bb % 2, d, :], ALU.add, ALU.add))
                    WG(nc.sync, s_oev, 8 * bb + 8)
                    s_ost[0].inc(nc.sync.dma_start(
                        outT_r[:, :, bb * NB:(bb + 1) * NB], outb[:, :, :]), 16)
                if b >= 1 and b + 1 < NBB:
                    t = b + 1
                    WG(nc.sync, s_lnx2, 8 * (t - 1))
                    WG(nc.sync, s_sq2p, 8 * (t - 1))
                    WG(nc.sync, s_cast2, 8 * (t - 1))
                    WG(nc.sync, s_oev, 8 * (t - 1))
                    s_xb[t % 2].inc(nc.sync.dma_start(
                        x1[:, t % 2, :, :], xT_r[:, :, t * NB:(t + 1) * NB]), 16)

            WG(nc.sync, s_ost[0], 16 * NBB)

        for s in allsems:
            assert s.maxwait <= s.n, f"wait {s.maxwait} > final {s.n} on {s.name}"
    return nc


_NC_CACHE = None


def kernel(L0_attn_x, L0_attn_w_qkv, L0_attn_w_proj, L0_attn_b_proj,
           L0_ffn_w1, L0_ffn_b1, L0_ffn_w2, L0_ffn_b2):
    global _NC_CACHE
    x = np.asarray(L0_attn_x, dtype=np.float32)
    w_qkv = np.asarray(L0_attn_w_qkv, dtype=np.float32)
    w_proj = np.asarray(L0_attn_w_proj, dtype=np.float32)
    b_proj = np.asarray(L0_attn_b_proj, dtype=np.float32)
    w1 = np.asarray(L0_ffn_w1, dtype=np.float32)
    b1 = np.asarray(L0_ffn_b1, dtype=np.float32)
    w2 = np.asarray(L0_ffn_w2, dtype=np.float32)
    b2 = np.asarray(L0_ffn_b2, dtype=np.float32)
    assert x.shape == (B, N, C) and w_qkv.shape == (3 * C, C)

    wqkvT = np.ascontiguousarray(w_qkv.T)
    wprojT = np.ascontiguousarray(w_proj.T)
    bprojc = np.ascontiguousarray(b_proj.reshape(CK, 128).T)
    b1c = np.ascontiguousarray(b1.reshape(WK, 128).T)
    b2c = np.ascontiguousarray(b2.reshape(CK, 128).T)
    w2r = np.ascontiguousarray(
        w2.reshape(WK, 128, CK, 128).transpose(2, 1, 0, 3)).reshape(CK * 128, WK, 128)
    zsel = np.zeros((16, CK, 128), dtype=np.float32)
    for p in range(CK):
        zsel[2 * p, p, 0:64] = 1.0
        zsel[2 * p + 1, p, 64:128] = 1.0

    if _NC_CACHE is None:
        _NC_CACHE = build_nc()
    nc = _NC_CACHE

    in_maps = []
    for bb in range(B):
        in_maps.append({
            "xT": np.ascontiguousarray(x[bb].T),
            "wqkvT": wqkvT, "wprojT": wprojT, "bprojc": bprojc,
            "w1": w1, "b1c": b1c, "w2r": w2r, "b2c": b2c, "zsel": zsel,
        })
    res = run_bass_kernel_spmd(nc, in_maps, core_ids=list(range(B)))
    out = np.stack([np.ascontiguousarray(r["outT"].T) for r in res.results], axis=0)
    return out
